# version 03
# baseline (speedup 1.0000x reference)
import numpy as np
import concourse.bass as bass
import concourse.mybir as mybir
from concourse.bass_utils import run_bass_kernel_spmd
from contextlib import ExitStack

F32 = mybir.dt.float32
BF16 = mybir.dt.bfloat16
AF = mybir.ActivationFunctionType
ALU = mybir.AluOpType
AX = mybir.AxisListType.X

NT = 8
KVH = 8
NH = 32
DCH = 32
RSQ = float(1.0 / np.sqrt(np.float32(128.0)))
NEG = -1.0e30
G16 = 1.0 / 16.0

ENABLE_ATTN = True
ATTN_STAGE = 4
DEBUG = 0

ENGS = ("pe", "dve", "act", "sq", "gq")


class KB:
    def __init__(self, nc, ctx):
        self.nc = nc
        self.ctx = ctx
        self.ops = {e: [] for e in ENGS}
        self.sems = {}
        self.cnt = {}

    def sem(self, name):
        if name not in self.sems:
            self.sems[name] = self.ctx.enter_context(self.nc.semaphore(name))
            self.cnt[name] = 0
        return self.sems[name]

    def wait(self, e, name, val=None):
        s = self.sem(name)
        v = self.cnt[name] if val is None else val
        if v <= 0:
            return
        self.ops[e].append(lambda eng, s=s, v=v: eng.wait_ge(s, v))

    def op(self, e, fn, mark=None):
        if mark is None:
            self.ops[e].append(lambda eng, fn=fn: fn())
            return None
        s = self.sem(mark)
        self.cnt[mark] += 1
        self.ops[e].append(lambda eng, fn=fn, s=s: fn().then_inc(s, 1))
        return self.cnt[mark]

    def fop(self, e, fn):
        v = self.op(e, fn, mark=e)
        self.wait(e, e, v)
        return v

    def dma(self, e, out, in_, mark):
        s = self.sem(mark)
        self.cnt[mark] += 16
        self.ops[e].append(
            lambda eng, out=out, in_=in_, s=s: eng.dma_start(out=out, in_=in_).then_inc(s, 16))
        return self.cnt[mark]

    def barrier(self):
        snap = [(n, self.cnt[n]) for n in list(self.cnt) if self.cnt[n] > 0]
        for e in ENGS:
            for n, v in snap:
                self.wait(e, n, v)

    def emit(self):
        with self.nc.Block() as block:
            decs = {"sq": block.sync, "gq": block.gpsimd, "pe": block.tensor,
                    "dve": block.vector, "act": block.scalar}
            for e in ENGS:
                def body(eng, e=e):
                    for f in self.ops[e]:
                        f(eng)
                decs[e](body)


def build_program():
    nc = bass.Bass("TRN2", target_bir_lowering=False)
    ctx = ExitStack()
    k = KB(nc, ctx)
    V, A_, P = nc.vector, nc.scalar, nc.tensor

    di = lambda n, s: nc.dram_tensor(n, s, F32, kind="ExternalInput").ap()
    x_own = di("x_own", [1024, 4096])
    x_pfx = di("x_pfx", [1024, 4096])
    Wq = di("Wq", [4096, 4096])
    Wk = di("Wk", [4096, 1024])
    Wv = di("Wv", [4096, 1024])
    Wo = di("Wo", [4096, 4096])
    WT = di("WT", [4096, 4096])
    Aft = di("Aft", [4096, 256])
    Bft = di("Bft", [256, 4096])
    biasbc = di("biasbc", [128, 4096])
    coefr = di("coefr", [128, 2])
    scalr = di("scalr", [128, 2])
    cosd = di("cosd", [1024, 64])
    sind = di("sind", [1024, 64])
    cospd = di("cospd", [128, 64])
    sinpd = di("sinpd", [128, 64])
    trimd = di("trimd", [128, 256])
    swastdd = di("swastdd", [128, 128])
    swac0d = di("swac0d", [128, 128])
    identd = di("identd", [128, 128])

    out_d = nc.dram_tensor("out", [1024, 4096], F32, kind="ExternalOutput").ap()
    scr_d = nc.dram_tensor("scr", [1024, 4096], F32, kind="Internal").ap()
    dbg_d = (nc.dram_tensor("dbg", [128, 40960], BF16, kind="ExternalOutput").ap()
             if DEBUG else None)

    sb = lambda n, s, d: ctx.enter_context(nc.sbuf_tensor(n, s, d))
    psf = lambda n: ctx.enter_context(nc.psum_tensor(n, [128, 512], F32))
    psb = lambda n: ctx.enter_context(nc.psum_tensor(n, [128, 1024], BF16))

    xnT = sb("xnT", [128, 32768], BF16)
    oT = sb("oT", [128, 32768], BF16)
    ar1 = sb("ar1", [128, 16384], BF16)
    R2 = sb("R2", [128, 14720], BF16)
    R2f = R2.ap().bitcast(F32)

    cosb = sb("cosb", [128, 576], F32)
    sinb = sb("sinb", [128, 576], F32)
    trim = sb("trim", [128, 256], F32)
    swastd = sb("swastd", [128, 128], F32)
    swac0 = sb("swac0", [128, 128], F32)
    idf = sb("idf", [128, 128], F32)
    idb = sb("idb", [128, 128], BF16)
    Sf = sb("Sf", [128, 128], F32)
    csw = sb("csw", [128, 8], F32)
    zeros64 = sb("zeros64", [128, 64], F32)
    khalo = sb("khalo", [128, 512], BF16)
    vhalo = sb("vhalo", [128, 1024], BF16)
    smallb = sb("smallb", [128, 1024], BF16)

    Spfx = oT.ap().bitcast(F32)[:, 15360:16384]

    PF = [psf(f"PF{i}") for i in range(6)]
    PB = [psb(f"PB{i}") for i in range(2)]

    c_v = 0
    c_kt = 1024
    c_kr = 2048
    c_eb = 3072
    c_sb = 4096
    c_qt = 6144
    c_qr = 8192
    f_qraw = 5120
    f_sA = 5376
    f_sB = 5504
    f_sC = 5632
    f_sD = 5760
    f_B = 5888
    f_enB = 6016
    f_dec = 6144
    f_kr = 6272
    f_sc = 6528
    f_ep = 6656
    f_cols = 6784
    f_x2 = 6848
    c_ks = 13696
    c_kh = 13824
    c_qs = 13952
    c_pb = 14208
    c_Pb = 14336
    c_pT = 14464

    scol = lambda i: R2f[:, f_cols + i:f_cols + i + 1]

    def ln_tile(xsrc, tile, dst_ap):
        xb = R2f[:, 0:4096]
        k.wait("sq", "act")
        k.dma("sq", xb, xsrc[tile * 128:(tile + 1) * 128, :], "xq")
        k.wait("dve", "xq")
        k.wait("dve", "act")
        mcol = scol(0)
        k.fop("dve", lambda: V.reduce_sum(mcol, xb, axis=AX))
        k.fop("dve", lambda: V.tensor_scalar_mul(mcol, mcol, 1.0 / 4096.0))
        sq = R2f[:, f_x2:f_x2 + 512]
        ss = R2f[:, f_cols + 8:f_cols + 16]
        k.wait("act", "xq")
        k.wait("act", "dve")
        for w in range(8):
            k.op("act", lambda w=w: A_.activation(
                sq, xb[:, w * 512:(w + 1) * 512], AF.Square,
                accum_out=ss[:, w:w + 1]), mark="act" if w == 7 else None)
        k.wait("dve", "act")
        s2 = scol(1)
        k.fop("dve", lambda: V.reduce_sum(s2, ss, axis=AX))
        var = scol(2)
        k.fop("dve", lambda: V.tensor_scalar_mul(var, s2, 1.0 / 4096.0))
        m2 = scol(3)
        k.fop("dve", lambda: V.tensor_mul(m2, mcol, mcol))
        k.fop("dve", lambda: V.tensor_sub(var, var, m2))
        k.fop("dve", lambda: V.tensor_scalar_add(var, var, 1.0e-5))
        k.wait("act", "dve")
        sd = scol(4)
        k.op("act", lambda: A_.activation(sd, var, AF.Sqrt), mark="act")
        k.wait("dve", "act")
        rstd = scol(5)
        k.fop("dve", lambda: V.reciprocal(rstd, sd))
        nmr = scol(6)
        k.fop("dve", lambda: V.tensor_mul(nmr, mcol, rstd))
        k.fop("dve", lambda: V.tensor_scalar_mul(nmr, nmr, -1.0))
        k.wait("act", "dve")
        k.wait("act", "pe")
        k.op("act", lambda: A_.activation(dst_ap, xb, AF.Identity,
                                          bias=nmr, scale=rstd), mark="act")

    tp_war = {0: 0, 1: 0}

    def transpose32(src_bf16, dst_tensor, stride, off):
        k.wait("pe", "act")
        dstv = dst_tensor.ap().rearrange("p (a q) -> p a q", q=stride)
        for grp in range(8):
            bank = PB[grp % 2]
            if tp_war[grp % 2]:
                k.wait("pe", "dve", tp_war[grp % 2])
            for j in range(4):
                c = grp * 4 + j
                k.op("pe", lambda bank=bank, j=j, c=c, src=src_bf16: P.transpose(
                    bank[:, j * 128:(j + 1) * 128], src[:, c * 128:(c + 1) * 128], idb[:]),
                    mark="pe" if j == 3 else None)
            k.wait("dve", "pe")
            tp_war[grp % 2] = k.op(
                "dve", lambda bank=bank, grp=grp: V.tensor_copy(
                    dstv[:, grp * 4:grp * 4 + 4, off:off + 128],
                    bank[:, 0:512].rearrange("p (a b) -> p a b", b=128)),
                mark="dve")

    def rope6(src, dst, co, si, tmpa, tmpb):
        k.op("dve", lambda: V.tensor_tensor(tmpa[:, 0:64], src[:, 0:64], co, ALU.mult))
        k.op("dve", lambda: V.tensor_tensor(tmpa[:, 64:128], src[:, 64:128], si, ALU.mult))
        k.op("dve", lambda: V.tensor_tensor(tmpb[:, 0:64], src[:, 64:128], co, ALU.mult))
        k.fop("dve", lambda: V.tensor_tensor(tmpb[:, 64:128], src[:, 0:64], si, ALU.mult))
        k.op("dve", lambda: V.tensor_sub(dst[:, 0:64], tmpa[:, 0:64], tmpa[:, 64:128]))
        k.fop("dve", lambda: V.tensor_add(dst[:, 64:128], tmpb[:, 0:64], tmpb[:, 64:128]))

    def k_decor(g, t, v_ap, chunk_base, own, halo):
        kraw = R2f[:, f_qraw:f_qraw + 128]
        k.wait("dve", "act")
        k.fop("dve", lambda: V.reduce_max(scol(16), kraw, axis=AX, negate=True))
        k.wait("act", "dve")
        ek = R2f[:, f_sA:f_sA + 128]
        k.op("act", lambda: A_.activation(ek, kraw, AF.Exp, bias=scol(16),
                                          accum_out=scol(17)), mark="act")
        k.wait("dve", "act")
        k.fop("dve", lambda: V.reciprocal(scol(18), scol(17)))
        ks = R2[:, c_ks:c_ks + 128]
        k.fop("dve", lambda: V.tensor_scalar(ks, ek, scol(18), None, ALU.mult))
        sB = R2f[:, f_sB:f_sB + 128]
        sC = R2f[:, f_sC:f_sC + 128]
        sD = R2f[:, f_sD:f_sD + 128]
        k.fop("act", lambda: A_.activation(sB, kraw, AF.Abs))
        k.fop("act", lambda: A_.activation(sC, sB, AF.Exp, scale=-1.0))
        k.op("act", lambda: A_.activation(sB, sC, AF.Ln, bias=1.0), mark="act")
        k.wait("dve", "act")
        k.fop("dve", lambda: V.tensor_scalar_min(sC, kraw, 0.0))
        k.fop("dve", lambda: V.tensor_sub(sD, sC, sB))
        k.wait("pe", "dve")
        k.op("pe", lambda: P.transpose(PF[2][:, 0:128], sD, idf[:]), mark="pe")
        k.wait("dve", "pe")
        gsT = R2f[:, f_sA:f_sA + 128]
        Bc = R2f[:, f_B:f_B + 128]
        k.fop("dve", lambda: V.tensor_copy(gsT, PF[2][:, 0:128]))
        k.fop("dve", lambda: V.tensor_tensor_scan(
            Bc[:, 0:64], gsT[:, 0:64], zeros64[:], 0.0, ALU.add, ALU.add))
        k.fop("dve", lambda: V.tensor_tensor_scan(
            Bc[:, 64:128], gsT[:, 64:128], zeros64[:], 0.0, ALU.add, ALU.add))
        k.op("dve", lambda: V.tensor_scalar(scol(19), Bc[:, 63:64], G16, None, ALU.mult))
        k.fop("dve", lambda: V.tensor_scalar(scol(20), Bc[:, 127:128], G16, None, ALU.mult))
        k.wait("act", "dve")
        enB = R2f[:, f_enB:f_enB + 128]
        dec = R2f[:, f_dec:f_dec + 128]
        if own:
            k.op("act", lambda: A_.activation(R2[:, c_eb + t * 128:c_eb + (t + 1) * 128],
                                              Bc, AF.Exp, scale=G16))
            k.op("act", lambda: A_.activation(enB, Bc, AF.Exp, scale=-G16))
        k.op("act", lambda: A_.activation(dec[:, 0:64], Bc[:, 0:64], AF.Exp,
                                          bias=scol(19), scale=-G16))
        k.op("act", lambda: A_.activation(dec[:, 64:128], Bc[:, 64:128], AF.Exp,
                                          bias=scol(20), scale=-G16))
        k.op("act", lambda: A_.activation(scol(21), scol(19), AF.Exp))
        k.op("act", lambda: A_.activation(scol(22), scol(20), AF.Exp), mark="act")
        k.wait("pe", "act")
        k.op("pe", lambda: P.transpose(PF[2][:, 128:256], dec, idf[:]), mark="pe")
        k.wait("dve", "pe")
        kh = R2[:, c_kh:c_kh + 128]
        k.op("dve", lambda: V.tensor_tensor(kh, PF[2][:, 128:256], ks, ALU.mult),
             mark="dve")
        if own:
            k.wait("pe", "dve")
            k.op("pe", lambda: P.transpose(PB[0][:, 0:128], ks, idb[:]), mark="pe")
            k.wait("dve", "pe")
            k.op("dve", lambda: V.tensor_tensor(
                R2[:, c_kt + t * 128:c_kt + (t + 1) * 128], PB[0][:, 0:128], enB,
                ALU.mult), mark="dve")
        if own or halo:
            co = cosb[:, 512:576] if halo else cosb[:, t * 64:(t + 1) * 64]
            si = sinb[:, 512:576] if halo else sinb[:, t * 64:(t + 1) * 64]
            kr = R2f[:, f_kr:f_kr + 128]
            rope6(kraw, kr, co, si, sC, R2f[:, f_kr + 128:f_kr + 256])
            k.wait("pe", "dve")
            k.op("pe", lambda: P.transpose(PF[2][:, 256:384], kr, idf[:]), mark="pe")
            k.wait("dve", "pe")
            if own:
                k.op("dve", lambda: V.tensor_copy(
                    R2[:, c_kr + t * 128:c_kr + (t + 1) * 128], PF[2][:, 256:384]),
                    mark="dve")
            else:
                k.op("dve", lambda: V.tensor_copy(
                    khalo[:, g * 64:(g + 1) * 64], PF[2][:, 320:384]), mark="dve")
        if halo:
            k.op("dve", lambda: V.tensor_copy(
                vhalo[:, g * 128:(g + 1) * 128], v_ap), mark="dve")
        for cc in range(2):
            c = chunk_base + cc
            if own:
                k.op("dve", lambda c=c: V.tensor_copy(
                    R2[:, c_sb + c * 128:c_sb + (c + 1) * 128], Sf[:]), mark="dve")
            k.wait("pe", "dve")
            k.op("pe", lambda cc=cc, v_ap=v_ap: P.matmul(
                PF[3][:, 0:128], R2[64 * cc:64 * cc + 64, c_kh:c_kh + 128],
                v_ap[64 * cc:64 * cc + 64, :], start=True, stop=True), mark="pe")
            k.wait("dve", "pe")
            k.fop("dve", lambda cc=cc: V.scalar_tensor_tensor(
                Sf[:], Sf[:], scol(21 + cc), PF[3][:, 0:128], ALU.mult, ALU.add))

    k.dma("sq", cosb.ap().rearrange("p (t n) -> p t n", n=64)[:, 0:8, :],
          cosd.rearrange("(t p) n -> p t n", p=128), "tbl")
    k.dma("sq", sinb.ap().rearrange("p (t n) -> p t n", n=64)[:, 0:8, :],
          sind.rearrange("(t p) n -> p t n", p=128), "tbl")
    k.dma("sq", cosb[:, 512:576], cospd, "tbl")
    k.dma("sq", sinb[:, 512:576], sinpd, "tbl")
    k.dma("sq", trim[:], trimd, "tbl")
    k.dma("sq", swastd[:], swastdd, "tbl")
    k.dma("sq", swac0[:], swac0d, "tbl")
    k.dma("sq", idf[:], identd, "tbl")
    k.dma("gq", idb[:], identd, "gtbl")
    k.dma("sq", csw[:, 2:4], coefr, "tbl")
    k.dma("sq", csw[:, 4:6], scalr, "tbl")
    k.wait("dve", "tbl")
    k.wait("dve", "gtbl")
    k.fop("dve", lambda: V.tensor_mul(csw[:, 0:2], csw[:, 2:4], csw[:, 4:6]))
    k.fop("dve", lambda: V.memset(zeros64[:], 0.0))
    k.fop("dve", lambda: V.memset(Sf[:], 0.0))
    k.fop("dve", lambda: V.memset(oT[:], 0.0))
    k.barrier()

    for t in range(NT):
        xn_tok = R2[:, 8192:12288]
        ln_tile(x_own, t, xn_tok)
        transpose32(xn_tok, xnT, 1024, t * 128)
    k.barrier()
    if DEBUG:
        k.dma("sq", dbg_d[:, 0:32768], xnT[:], "oo")
        k.barrier()

    if ENABLE_ATTN:
        wslot0 = oT.ap()[:, 0:16384]
        for halfp in range(2):
            for th in range(4):
                t = halfp * 4 + th
                xn_tok = R2[:, 8192:12288]
                ln_tile(x_pfx, t, xn_tok)
                transpose32(xn_tok, ar1, 512, th * 128)
            k.barrier()
            for cb in range(4):
                wsrc = [Wv, Wv, Wk, Wk][cb]
                coff = [0, 512, 0, 512][cb]
                k.wait("gq", "pe")
                k.dma("gq", wslot0.rearrange("p (c n) -> p c n", n=512),
                      wsrc.rearrange("(c p) n -> p c n", p=128)[:, :, coff:coff + 512],
                      "wq")
                k.wait("pe", "wq")
                for th in range(4):
                    bank = PF[th % 2]
                    k.wait("pe", "dve")
                    for c in range(DCH):
                        k.op("pe", lambda bank=bank, c=c, th=th: P.matmul(
                            bank[:, 0:512],
                            ar1[:, c * 512 + th * 128:c * 512 + (th + 1) * 128],
                            wslot0.rearrange("p (c n) -> p c n", n=512)[:, c, :],
                            start=(c == 0), stop=(c == DCH - 1)),
                            mark="pe" if c == DCH - 1 else None)
                    k.wait("dve", "pe")
                    if cb < 2:
                        k.op("dve", lambda bank=bank, th=th, cb=cb: V.tensor_copy(
                            R2[:, th * 1024 + cb * 512:th * 1024 + cb * 512 + 512],
                            bank[:, 0:512]), mark="dve")
                    else:
                        for gg in range(4):
                            g = gg + (cb - 2) * 4
                            k.fop("dve", lambda bank=bank, gg=gg: V.tensor_copy(
                                R2f[:, f_qraw:f_qraw + 128],
                                bank[:, gg * 128:(gg + 1) * 128]))
                            t = halfp * 4 + th
                            v_ap = R2[:, th * 1024 + g * 128:th * 1024 + (g + 1) * 128]
                            k.fop("dve", lambda g=g: V.tensor_copy(
                                Sf[:], Spfx[:, g * 128:(g + 1) * 128]))
                            k_decor(g, t, v_ap, 0, own=False, halo=(t == 7))
                            k.fop("dve", lambda g=g: V.tensor_copy(
                                Spfx[:, g * 128:(g + 1) * 128], Sf[:]))
            k.barrier()

        wring = [ar1.ap()[:, 0:8192], ar1.ap()[:, 8192:16384]]
        wtok = [0, 0]
        wsem = ["wr0", "wr1"]
        ring_i = [0]

        def wload(srcs):
            p = ring_i[0] % 2
            ring_i[0] += 1
            if wtok[p]:
                k.wait("gq", "pe", wtok[p])
            slot = wring[p]
            for src3, dcol, dw in srcs:
                k.dma("gq", slot.rearrange("p (c n) -> p c n", n=256)[:, :, dcol:dcol + dw],
                      src3, wsem[p])
            return p, slot

        Wq3 = Wq.rearrange("(c p) n -> p c n", p=128)
        Wk3 = Wk.rearrange("(c p) n -> p c n", p=128)
        Wv3 = Wv.rearrange("(c p) n -> p c n", p=128)

        def proj_tile(slot, t, width, bank):
            for c in range(DCH):
                k.op("pe", lambda bank=bank, c=c, t=t, slot=slot, width=width: P.matmul(
                    bank[:, 0:width],
                    xnT[:, c * 1024 + t * 128:c * 1024 + (t + 1) * 128],
                    slot.rearrange("p (c n) -> p c n", n=256)[:, c, 0:width],
                    start=(c == 0), stop=(c == DCH - 1)),
                    mark="pe" if c == DCH - 1 else None)

        pkv, skv = wload([(Wk3[:, :, 0:128], 0, 128), (Wv3[:, :, 0:128], 128, 128)])
        for g in range(KVH if ATTN_STAGE >= 2 else 0):
            k.fop("dve", lambda g=g: V.tensor_copy(Sf[:], Spfx[:, g * 128:(g + 1) * 128]))
            k.wait("pe", wsem[pkv])
            for t in range(NT):
                bank = PF[t % 2]
                k.wait("pe", "dve")
                proj_tile(skv, t, 256, bank)
                k.wait("dve", "pe")
                k.fop("dve", lambda bank=bank: V.tensor_copy(
                    R2f[:, f_qraw:f_qraw + 128], bank[:, 0:128]))
                v_ap = R2[:, c_v + t * 128:c_v + (t + 1) * 128]
                k.op("dve", lambda bank=bank, v_ap=v_ap: V.tensor_copy(
                    v_ap, bank[:, 128:256]), mark="dve")
                k_decor(g, t, v_ap, 2 * t, own=True, halo=False)
            kv_done = k.cnt["pe"]
            wtok[pkv] = kv_done
            if ATTN_STAGE < 3:
                if g + 1 < KVH:
                    pkv, skv = wload([(Wk3[:, :, (g + 1) * 128:(g + 2) * 128], 0, 128),
                                      (Wv3[:, :, (g + 1) * 128:(g + 2) * 128], 128, 128)])
                continue
            pq0, sq0 = wload([(Wq3[:, :, g * 512:g * 512 + 256], 0, 256)])
            for hp in range(2 if ATTN_STAGE >= 3 else 0):
                psl, ssl = (pq0, sq0) if hp == 0 else wload(
                    [(Wq3[:, :, g * 512 + 256:g * 512 + 512], 0, 256)])
                k.wait("pe", wsem[psl])
                for t in range(NT):
                    bank = PF[t % 2]
                    k.wait("pe", "dve")
                    proj_tile(ssl, t, 256, bank)
                    k.wait("dve", "pe")
                    qraw = R2f[:, f_qraw:f_qraw + 256]
                    k.fop("dve", lambda bank=bank, qraw=qraw: V.tensor_copy(qraw, bank[:, 0:256]))
                    k.op("dve", lambda qraw=qraw: V.reduce_max(
                        scol(24), qraw[:, 0:128], axis=AX, negate=True))
                    k.fop("dve", lambda qraw=qraw: V.reduce_max(
                        scol(25), qraw[:, 128:256], axis=AX, negate=True))
                    k.wait("act", "dve")
                    eq = R2f[:, f_sA:f_sA + 256]
                    k.op("act", lambda qraw=qraw, eq=eq: A_.activation(
                        eq[:, 0:128], qraw[:, 0:128], AF.Exp, bias=scol(24),
                        accum_out=scol(26)))
                    k.op("act", lambda qraw=qraw, eq=eq: A_.activation(
                        eq[:, 128:256], qraw[:, 128:256], AF.Exp, bias=scol(25),
                        accum_out=scol(27)), mark="act")
                    k.wait("dve", "act")
                    k.fop("dve", lambda: V.reciprocal(R2f[:, f_cols + 28:f_cols + 30],
                                                      R2f[:, f_cols + 26:f_cols + 28]))
                    k.fop("dve", lambda: V.tensor_scalar(
                        R2f[:, f_cols + 28:f_cols + 30],
                        R2f[:, f_cols + 28:f_cols + 30], 0.5, None, ALU.mult))
                    qs = R2[:, c_qs:c_qs + 256]
                    k.op("dve", lambda eq=eq, qs=qs: V.tensor_scalar(
                        qs[:, 0:128], eq[:, 0:128], scol(28), None, ALU.mult))
                    k.op("dve", lambda eq=eq, qs=qs: V.tensor_scalar(
                        qs[:, 128:256], eq[:, 128:256], scol(29), None, ALU.mult),
                        mark="dve")
                    k.wait("pe", "dve")
                    for h2 in range(2):
                        k.op("pe", lambda h2=h2, qs=qs: P.transpose(
                            PB[1][:, h2 * 128:(h2 + 1) * 128],
                            qs[:, h2 * 128:(h2 + 1) * 128], idb[:]),
                            mark="pe" if h2 == 1 else None)
                    k.wait("dve", "pe")
                    for h2 in range(2):
                        k.op("dve", lambda h2=h2, t=t: V.tensor_tensor(
                            R2[:, c_qt + h2 * 1024 + t * 128:c_qt + h2 * 1024 + (t + 1) * 128],
                            PB[1][:, h2 * 128:(h2 + 1) * 128],
                            R2[:, c_eb + t * 128:c_eb + (t + 1) * 128], ALU.mult),
                            mark="dve" if h2 == 1 else None)
                    co = cosb[:, t * 64:(t + 1) * 64]
                    si = sinb[:, t * 64:(t + 1) * 64]
                    kr = R2f[:, f_kr:f_kr + 256]
                    for h2 in range(2):
                        rope6(qraw[:, h2 * 128:(h2 + 1) * 128],
                              kr[:, h2 * 128:(h2 + 1) * 128], co, si,
                              R2f[:, f_sC:f_sC + 128], R2f[:, f_sD:f_sD + 128])
                    k.wait("pe", "dve")
                    for h2 in range(2):
                        k.op("pe", lambda h2=h2, kr=kr: P.transpose(
                            PF[4][:, h2 * 128:(h2 + 1) * 128],
                            kr[:, h2 * 128:(h2 + 1) * 128], idf[:]),
                            mark="pe" if h2 == 1 else None)
                    k.wait("dve", "pe")
                    qrW = R2.ap()[:, c_qr:c_qr + 2048].rearrange("p (cc q) -> p cc q", q=128)
                    for h2 in range(2):
                        k.op("dve", lambda h2=h2, t=t, qrW=qrW: V.tensor_copy(
                            qrW[:, 2 * t:2 * t + 2, h2 * 64:(h2 + 1) * 64],
                            PF[4][:, h2 * 128:(h2 + 1) * 128].rearrange(
                                "p (a u) -> p a u", u=64)),
                            mark="dve" if h2 == 1 else None)
                wtok[psl] = k.cnt["pe"]
                if hp == 1 and g + 1 < KVH:
                    pkv, skv = wload([(Wk3[:, :, (g + 1) * 128:(g + 2) * 128], 0, 128),
                                      (Wv3[:, :, (g + 1) * 128:(g + 2) * 128], 128, 128)])
                qtv = R2.ap()[:, c_qt:c_qt + 2048].rearrange("p (a q) -> p a q", q=1024)
                qt3 = lambda c: qtv[:, :, c * 64:(c + 1) * 64]
                otv = oT.ap().rearrange("p (a q) -> p a q", q=1024)
                oev = {0: 0, 1: 0}
                tok = {}
                pbv = [R2[:, c_pb:c_pb + 128], R2[:, c_qs:c_qs + 128]]
                pTa = R2[:, c_pT:c_pT + 128]
                pTb = R2[:, c_pT + 128:c_pT + 256]
                Pb = R2[:, c_Pb:c_Pb + 128]

                qr3 = lambda c: R2[:, c_qr + c * 128:c_qr + (c + 1) * 128]

                def em_scores(c):
                    k.wait("pe", "dve", tok.get(("stt", c - 2), 0))
                    if c == 0:
                        k.op("pe", lambda g=g: P.matmul(
                            PF[0][:, 0:64], qr3(0),
                            khalo[:, g * 64:(g + 1) * 64], start=True, stop=True))
                        k.op("pe", lambda: P.matmul(
                            PF[0][:, 64:128], qr3(0),
                            R2[:, c_kr:c_kr + 64], start=True, stop=True), mark="pe")
                    else:
                        k.op("pe", lambda c=c: P.matmul(
                            PF[c % 2][:, 0:128], qr3(c),
                            R2[:, c_kr + (c - 1) * 64:c_kr + (c + 1) * 64],
                            start=True, stop=True), mark="pe")
                    tok[("scpe", c)] = k.cnt["pe"]

                def em_softmax(c):
                    k.wait("dve", "pe", tok[("scpe", c)])
                    if ("exp", c - 1) in tok:
                        k.wait("dve", "act", tok[("exp", c - 1)])
                    msk = swac0 if c == 0 else swastd
                    sc_ = R2f[:, f_sc:f_sc + 128]
                    k.fop("dve", lambda sc_=sc_, msk=msk, c=c: V.scalar_tensor_tensor(
                        sc_, PF[c % 2][:, 0:128], RSQ, msk[:], ALU.mult, ALU.add))
                    tok[("stt", c)] = k.cnt["dve"]
                    k.fop("dve", lambda sc_=sc_: V.reduce_max(
                        scol(32), sc_, axis=AX, negate=True))
                    k.wait("act", "dve")
                    ep = R2f[:, f_ep:f_ep + 128]
                    tok[("exp", c)] = k.op("act", lambda ep=ep, sc_=sc_: A_.activation(
                        ep, sc_, AF.Exp, bias=scol(32), accum_out=scol(33)), mark="act")
                    k.wait("dve", "act", tok[("exp", c)])
                    k.fop("dve", lambda: V.reciprocal(scol(34), scol(33)))
                    k.fop("dve", lambda: V.tensor_scalar(scol(34), scol(34), 0.5, None, ALU.mult))
                    tok[("p", c)] = k.op("dve", lambda c=c, ep=ep: V.tensor_scalar(
                        pbv[c % 2], ep, scol(34), None, ALU.mult), mark="dve")

                em_scores(0)
                em_softmax(0)
                for c in range(16):
                    t = c // 2
                    cp = c % 2
                    if c + 1 < 16:
                        em_scores(c + 1)
                    k.wait("pe", "dve", max(tok[("p", c)], tok.get(("ptc", c - 1), 0)))
                    pb = pbv[c % 2]
                    if cp == 1:
                        k.op("pe", lambda pb=pb: P.transpose(PB[0][:, 0:128], pb, idb[:]),
                             mark="pe")
                    else:
                        k.op("pe", lambda pb=pb: P.transpose(
                            PB[0][64:128, 0:128], pb[:, 0:64], idb[:]))
                        k.op("pe", lambda pb=pb: P.transpose(
                            PB[0][0:64, 0:128], pb[:, 64:128], idb[:]), mark="pe")
                    tok[("ptpe", c)] = k.cnt["pe"]
                    k.wait("pe", "dve", tok.get(("mask", c - 1), 0))
                    klo = (c - 1) * 64 if cp == 1 else c * 64
                    k.op("pe", lambda c=c, klo=klo: P.matmul(
                        PF[4][:, 0:128],
                        R2[:, c_kt + klo:c_kt + klo + 128], qt3(c),
                        start=True, stop=True), mark="pe")
                    tok[("phpe", c)] = k.cnt["pe"]
                    k.wait("dve", "pe", tok[("ptpe", c)])
                    if cp == 1:
                        k.op("dve", lambda: V.tensor_copy(pTa[:], PB[0][:, 0:128]),
                             mark="dve")
                    else:
                        k.op("dve", lambda: V.memset(pTa[0:64, :], 0.0))
                        k.op("dve", lambda: V.tensor_copy(pTa[64:128, :], PB[0][64:128, 0:128]))
                        k.op("dve", lambda: V.tensor_copy(pTb[0:64, :], PB[0][0:64, 0:128]))
                        k.op("dve", lambda: V.memset(pTb[64:128, :], 0.0), mark="dve")
                    tok[("ptc", c)] = k.cnt["dve"]
                    k.wait("dve", "pe", tok[("phpe", c)])
                    trv = trim[:, 0:128] if cp == 0 else trim[:, 128:256]
                    tok[("mask", c)] = k.op("dve", lambda trv=trv: V.tensor_tensor(
                        Pb[:], PF[4][:, 0:128], trv, ALU.mult), mark="dve")
                    if c + 1 < 16:
                        em_softmax(c + 1)
                    ob = PF[2 + (c % 2)]
                    k.wait("pe", "dve", max(tok[("mask", c)], oev[c % 2]))
                    vt = R2[:, c_v + t * 128:c_v + (t + 1) * 128]
                    k.op("pe", lambda ob=ob, vt=vt: P.matmul(
                        ob[:, 0:128], vt, Pb[:], start=True, stop=False))
                    k.op("pe", lambda ob=ob, c=c: P.matmul(
                        ob[:, 0:128], R2[:, c_sb + c * 128:c_sb + (c + 1) * 128], qt3(c),
                        start=False, stop=False))
                    if cp == 1:
                        k.op("pe", lambda ob=ob, vt=vt: P.matmul(
                            ob[:, 0:128], vt, pTa[:], start=False, stop=True), mark="pe")
                    else:
                        vprev = (vhalo[:, g * 128:(g + 1) * 128] if c == 0 else
                                 R2[:, c_v + (t - 1) * 128:c_v + t * 128])
                        k.op("pe", lambda ob=ob, vprev=vprev: P.matmul(
                            ob[:, 0:128], vprev, pTa[:], start=False, stop=False))
                        k.op("pe", lambda ob=ob, vt=vt: P.matmul(
                            ob[:, 0:128], vt, pTb[:], start=False, stop=True), mark="pe")
                    tok[("ope", c)] = k.cnt["pe"]
                    k.wait("dve", "pe", tok[("ope", c)])
                    oev[c % 2] = k.op("dve", lambda ob=ob, g=g, hp=hp, c=c: V.tensor_copy(
                        otv[:, 4 * g + 2 * hp:4 * g + 2 * hp + 2, c * 64:(c + 1) * 64],
                        ob[:, 0:128].rearrange("p (a b) -> p a b", b=64)), mark="dve")
        k.barrier()

    k.dma("gq", ar1.ap()[:, 0:8192].rearrange("p (c n) -> p c n", n=256),
          Aft.rearrange("(c p) n -> p c n", p=128), "wr0")
    c_xa = 12288
    k.wait("pe", "wr0")
    k.wait("pe", "dve")
    for e2 in range(2):
        for th in range(2):
            bank = PF[(e2 * 2 + th) % 2]
            k.wait("pe", "dve")
            for c in range(DCH):
                k.op("pe", lambda bank=bank, c=c, e2=e2, th=th: P.matmul(
                    bank[:, 0:512],
                    ar1[:, c * 256 + e2 * 128:c * 256 + e2 * 128 + 128],
                    xnT[:, c * 1024 + th * 512:c * 1024 + (th + 1) * 512],
                    start=(c == 0), stop=(c == DCH - 1)),
                    mark="pe" if c == DCH - 1 else None)
            k.wait("dve", "pe")
            k.op("dve", lambda bank=bank, e2=e2, th=th: V.tensor_scalar(
                R2[:, c_xa + e2 * 1024 + th * 512:c_xa + e2 * 1024 + (th + 1) * 512],
                bank[:, 0:512], csw[:, e2:e2 + 1], None, ALU.mult), mark="dve")
    k.barrier()

    wslotA = [ar1.ap()[:, 0:8192], ar1.ap()[:, 8192:16384]]
    bfslot = [smallb.ap()[:, 0:512], smallb.ap()[:, 512:1024]]
    bslot = [R2.ap()[:, 2048:2304], R2.ap()[:, 2304:2560]]
    wtokA = [0, 0]
    xring = [R2f[:, 0:256], R2f[:, 256:512]]
    ering = [R2f[:, 512:768], R2f[:, 768:1024]]
    xrtok = [0, 0]
    WT3 = WT.rearrange("(c p) n -> p c n", p=128)
    Bf3 = Bft.rearrange("(e p) n -> p e n", p=128)

    def load_wt(j):
        p = j % 2
        if wtokA[p]:
            k.wait("gq", "pe", wtokA[p])
        k.dma("gq", wslotA[p].rearrange("p (c n) -> p c n", n=256),
              WT3[:, :, j * 256:(j + 1) * 256], f"wr{p}")
        k.dma("gq", bfslot[p].rearrange("p (e n) -> p e n", n=256),
              Bf3[:, :, j * 256:(j + 1) * 256], f"wr{p}")
        k.dma("gq", bslot[p], biasbc[:, j * 256:(j + 1) * 256], f"wr{p}")

    load_wt(0)
    if DEBUG:
        k.wait("sq", "wr0")
        k.dma("sq", dbg_d[:, 32768:40960], wslotA[0], "oo")
        k.barrier()
    for j in range(16):
        p = j % 2
        if j + 1 < 16:
            load_wt(j + 1)
        k.wait("pe", f"wr{p}")
        for t in range(NT):
            bank = PF[2 + (t % 2)]
            if xrtok[t % 2]:
                k.wait("pe", "dve", xrtok[t % 2])
            for c in range(DCH):
                k.op("pe", lambda bank=bank, c=c, p=p, t=t: P.matmul(
                    bank[:, 0:256],
                    xnT[:, c * 1024 + t * 128:c * 1024 + (t + 1) * 128],
                    wslotA[p].rearrange("p (c n) -> p c n", n=256)[:, c, :],
                    start=(c == 0), stop=False))
            for e2 in range(2):
                k.op("pe", lambda bank=bank, e2=e2, p=p, t=t: P.matmul(
                    bank[:, 0:256],
                    R2[:, c_xa + e2 * 1024 + t * 128:c_xa + e2 * 1024 + (t + 1) * 128],
                    bfslot[p].rearrange("p (e n) -> p e n", n=256)[:, e2, :],
                    start=False, stop=(e2 == 1)),
                    mark="pe" if e2 == 1 else None)
            wtokA[p] = k.cnt["pe"]
            xp = t % 2
            if xrtok[xp]:
                k.wait("sq", "dve", xrtok[xp])
            k.dma("sq", xring[xp], x_own[t * 128:(t + 1) * 128, j * 256:(j + 1) * 256], "xq")
            k.wait("dve", "pe")
            k.wait("dve", "xq")
            k.fop("dve", lambda bank=bank, xp=xp: V.scalar_tensor_tensor(
                ering[xp], bank[:, 0:256], 1.0, xring[xp], ALU.mult, ALU.add))
            xrtok[xp] = k.op("dve", lambda xp=xp, p=p: V.tensor_tensor(
                ering[xp], ering[xp], bslot[p], ALU.add), mark="dve")
            k.wait("sq", "dve", xrtok[xp])
            k.dma("sq", scr_d[t * 128:(t + 1) * 128, j * 256:(j + 1) * 256], ering[xp], "so")
    k.barrier()

    woslot = [xnT.ap()[:, 0:16384], xnT.ap()[:, 16384:32768]]
    wotok = [0, 0]
    sring = [R2f[:, 0:512], R2f[:, 512:1024]]
    oring = [R2f[:, 2048:2560], R2f[:, 2560:3072]]
    srtok = [0, 0]
    Wo3 = Wo.rearrange("(c p) n -> p c n", p=128)

    def load_wo(j):
        p = j % 2
        if wotok[p]:
            k.wait("gq", "pe", wotok[p])
        k.dma("gq", woslot[p].rearrange("p (c n) -> p c n", n=512),
              Wo3[:, :, j * 512:(j + 1) * 512], f"wr{p}")

    load_wo(0)
    for j in range(8):
        p = j % 2
        if j + 1 < 8:
            load_wo(j + 1)
        k.wait("pe", f"wr{p}")
        for t in range(NT):
            bank = PF[2 + (t % 2)]
            if srtok[t % 2]:
                k.wait("pe", "dve", srtok[t % 2])
            for h in range(NH):
                k.op("pe", lambda bank=bank, h=h, p=p, t=t: P.matmul(
                    bank[:, 0:512],
                    oT[:, h * 1024 + t * 128:h * 1024 + (t + 1) * 128],
                    woslot[p].rearrange("p (c n) -> p c n", n=512)[:, h, :],
                    start=(h == 0), stop=(h == NH - 1)),
                    mark="pe" if h == NH - 1 else None)
            wotok[p] = k.cnt["pe"]
            rp = t % 2
            if srtok[rp]:
                k.wait("sq", "dve", srtok[rp])
            k.dma("sq", sring[rp], scr_d[t * 128:(t + 1) * 128, j * 512:(j + 1) * 512], "si")
            k.wait("dve", "pe")
            k.wait("dve", "si")
            srtok[rp] = k.op("dve", lambda bank=bank, rp=rp: V.scalar_tensor_tensor(
                oring[rp], bank[:, 0:512], 1.0, sring[rp], ALU.mult, ALU.add), mark="dve")
            k.wait("sq", "dve", srtok[rp])
            k.dma("sq", out_d[t * 128:(t + 1) * 128, j * 512:(j + 1) * 512], oring[rp], "oo")

    k.wait("sq", "oo")
    k.emit()
    return nc


_CACHE = {}


def kernel(**inputs):
    nc = _CACHE.get("nc")
    if nc is None:
        nc = build_program()
        _CACHE["nc"] = nc

    x = np.ascontiguousarray(np.asarray(inputs["L0_attn_hidden_states"], np.float32))
    Wq = np.ascontiguousarray(np.asarray(inputs["L0_attn_Wq"], np.float32))
    Wk = np.ascontiguousarray(np.asarray(inputs["L0_attn_Wk"], np.float32))
    Wv = np.ascontiguousarray(np.asarray(inputs["L0_attn_Wv"], np.float32))
    Wo = np.ascontiguousarray(np.asarray(inputs["L0_attn_Wo"], np.float32))
    WT = np.ascontiguousarray(np.asarray(inputs["L0_ffn_W"], np.float32).T)
    A = np.asarray(inputs["L0_ffn_A"], np.float32)
    Bf = np.asarray(inputs["L0_ffn_Bf"], np.float32)
    bias = np.asarray(inputs["L0_ffn_bias"], np.float32)
    coeffs = np.asarray(inputs["L0_ffn_coeffs"], np.float32)
    scales = np.asarray(inputs["L0_ffn_scales"], np.float32)

    Aft = np.ascontiguousarray(A.transpose(2, 0, 1).reshape(4096, 256))
    Bft = np.ascontiguousarray(Bf.transpose(0, 2, 1).reshape(256, 4096))
    biasbc = np.ascontiguousarray(np.tile(bias[None, :], (128, 1)))
    coefr = np.ascontiguousarray(np.repeat(coeffs, 16).reshape(2, 128).T)
    scalr = np.ascontiguousarray(np.repeat(scales, 16).reshape(2, 128).T)

    inv = (1.0 / (10000.0 ** (2.0 * np.arange(64, dtype=np.float64) / 128.0)))
    a = np.arange(64)[:, None]
    cc = np.arange(128)[None, :]
    rel = a + 64 - cc
    basem = (rel >= 0) & (rel <= 64)
    swastd = np.where(basem, 0.0, NEG).astype(np.float32)
    swastd = np.concatenate([swastd, swastd], axis=0)
    b0 = basem & (cc >= 64)
    swa_even = np.concatenate([np.where(b0, 0.0, NEG).astype(np.float32)] * 2, axis=0)
    tk = np.arange(64)[:, None]
    tq = np.arange(64)[None, :]
    tri = (tq >= tk).astype(np.float32)
    base = np.concatenate([tri, tri], axis=1)
    z64 = np.zeros((64, 128), np.float32)
    trimm = np.concatenate([np.concatenate([base, z64], axis=0),
                            np.concatenate([z64, base], axis=0)], axis=1)
    ident = np.eye(128, dtype=np.float32)
    zeros_pfx = np.zeros((1024, 4096), np.float32)
    zeros_tab = np.zeros((128, 64), np.float32)

    in_maps = []
    for c in range(8):
        b, half = c // 2, c % 2
        pos = (half * 1024 + np.arange(1024, dtype=np.float64))[:, None] * inv[None, :]
        cos = np.cos(pos).astype(np.float32)
        sin = np.sin(pos).astype(np.float32)
        if half == 1:
            ppos = (960 + np.arange(64, dtype=np.float64))[:, None] * inv[None, :]
            pc = np.concatenate([zeros_tab[0:64], np.cos(ppos).astype(np.float32)], axis=0)
            psn = np.concatenate([zeros_tab[0:64], np.sin(ppos).astype(np.float32)], axis=0)
        else:
            pc, psn = zeros_tab, zeros_tab
        xo = np.ascontiguousarray(x[b, half * 1024:(half + 1) * 1024, :])
        xp = np.ascontiguousarray(x[b, 0:1024, :]) if half == 1 else zeros_pfx
        in_maps.append({
            "x_own": xo, "x_pfx": xp,
            "Wq": Wq, "Wk": Wk, "Wv": Wv, "Wo": Wo, "WT": WT,
            "Aft": Aft, "Bft": Bft, "biasbc": biasbc,
            "coefr": coefr, "scalr": scalr,
            "cosd": np.ascontiguousarray(cos), "sind": np.ascontiguousarray(sin),
            "cospd": np.ascontiguousarray(pc), "sinpd": np.ascontiguousarray(psn),
            "trimd": trimm,
            "swastdd": swastd,
            "swac0d": swa_even if half == 0 else swastd,
            "identd": ident,
        })

    res = run_bass_kernel_spmd(nc, in_maps, core_ids=list(range(8)))
    if DEBUG:
        _CACHE["res"] = res
    out = np.empty((4, 2048, 4096), np.float32)
    for c in range(8):
        b, half = c // 2, c % 2
        out[b, half * 1024:(half + 1) * 1024, :] = res.results[c]["out"]
    return out


# version 04
# speedup vs baseline: 1.1824x; 1.1824x over previous
import numpy as np
import concourse.bass as bass
import concourse.mybir as mybir
from concourse.bass_utils import run_bass_kernel_spmd
from contextlib import ExitStack

F32 = mybir.dt.float32
BF16 = mybir.dt.bfloat16
AF = mybir.ActivationFunctionType
ALU = mybir.AluOpType
AX = mybir.AxisListType.X

NT = 8
KVH = 8
NH = 32
DCH = 32
RSQ = float(1.0 / np.sqrt(np.float32(128.0)))
NEG = -1.0e30
G16 = 1.0 / 16.0

ENABLE_ATTN = True
ATTN_STAGE = 4
DEBUG = 0

ENGS = ("pe", "dve", "act", "sq", "gq")


class KB:
    def __init__(self, nc, ctx):
        self.nc = nc
        self.ctx = ctx
        self.ops = {e: [] for e in ENGS}
        self.sems = {}
        self.cnt = {}

    def sem(self, name):
        if name not in self.sems:
            self.sems[name] = self.ctx.enter_context(self.nc.semaphore(name))
            self.cnt[name] = 0
        return self.sems[name]

    def wait(self, e, name, val=None):
        s = self.sem(name)
        v = self.cnt[name] if val is None else val
        if v <= 0:
            return
        self.ops[e].append(lambda eng, s=s, v=v: eng.wait_ge(s, v))

    def op(self, e, fn, mark=None):
        if mark is None:
            self.ops[e].append(lambda eng, fn=fn: fn())
            return None
        s = self.sem(mark)
        self.cnt[mark] += 1
        self.ops[e].append(lambda eng, fn=fn, s=s: fn().then_inc(s, 1))
        return self.cnt[mark]

    def fop(self, e, fn):
        v = self.op(e, fn, mark=e)
        self.wait(e, e, v)
        return v

    def dma(self, e, out, in_, mark):
        s = self.sem(mark)
        self.cnt[mark] += 16
        self.ops[e].append(
            lambda eng, out=out, in_=in_, s=s: eng.dma_start(out=out, in_=in_).then_inc(s, 16))
        return self.cnt[mark]

    def barrier(self):
        snap = [(n, self.cnt[n]) for n in list(self.cnt) if self.cnt[n] > 0]
        for e in ENGS:
            for n, v in snap:
                self.wait(e, n, v)

    def emit(self):
        with self.nc.Block() as block:
            decs = {"sq": block.sync, "gq": block.gpsimd, "pe": block.tensor,
                    "dve": block.vector, "act": block.scalar}
            for e in ENGS:
                def body(eng, e=e):
                    for f in self.ops[e]:
                        f(eng)
                decs[e](body)


def build_program():
    nc = bass.Bass("TRN2", target_bir_lowering=False)
    ctx = ExitStack()
    k = KB(nc, ctx)
    V, A_, P = nc.vector, nc.scalar, nc.tensor

    di = lambda n, s: nc.dram_tensor(n, s, F32, kind="ExternalInput").ap()
    x_own = di("x_own", [1024, 4096])
    x_pfx = di("x_pfx", [1024, 4096])
    Wq = di("Wq", [4096, 4096])
    Wk = di("Wk", [4096, 1024])
    Wv = di("Wv", [4096, 1024])
    Wo = di("Wo", [4096, 4096])
    WT = di("WT", [4096, 4096])
    Aft = di("Aft", [4096, 256])
    Bft = di("Bft", [256, 4096])
    biasbc = di("biasbc", [128, 4096])
    coefr = di("coefr", [128, 2])
    scalr = di("scalr", [128, 2])
    cosd = di("cosd", [1024, 64])
    sind = di("sind", [1024, 64])
    cospd = di("cospd", [128, 64])
    sinpd = di("sinpd", [128, 64])
    trimd = di("trimd", [128, 256])
    swastdd = di("swastdd", [128, 128])
    swac0d = di("swac0d", [128, 128])
    identd = di("identd", [128, 128])

    out_d = nc.dram_tensor("out", [1024, 4096], F32, kind="ExternalOutput").ap()
    scr_d = nc.dram_tensor("scr", [1024, 4096], F32, kind="Internal").ap()
    dbg_d = (nc.dram_tensor("dbg", [128, 40960], BF16, kind="ExternalOutput").ap()
             if DEBUG else None)

    sb = lambda n, s, d: ctx.enter_context(nc.sbuf_tensor(n, s, d))
    psf = lambda n: ctx.enter_context(nc.psum_tensor(n, [128, 512], F32))
    psb = lambda n: ctx.enter_context(nc.psum_tensor(n, [128, 1024], BF16))

    xnT = sb("xnT", [128, 32768], BF16)
    oT = sb("oT", [128, 32768], BF16)
    ar1 = sb("ar1", [128, 16384], BF16)
    R2 = sb("R2", [128, 14720], BF16)
    R2f = R2.ap().bitcast(F32)

    cosb = sb("cosb", [128, 576], F32)
    sinb = sb("sinb", [128, 576], F32)
    trim = sb("trim", [128, 256], F32)
    swastd = sb("swastd", [128, 128], F32)
    swac0 = sb("swac0", [128, 128], F32)
    idf = sb("idf", [128, 128], F32)
    idb = sb("idb", [128, 128], BF16)
    Sf = sb("Sf", [128, 128], F32)
    csw = sb("csw", [128, 8], F32)
    zeros64 = sb("zeros64", [128, 64], F32)
    khalo = sb("khalo", [128, 512], BF16)
    vhalo = sb("vhalo", [128, 1024], BF16)
    smallb = sb("smallb", [128, 1024], BF16)

    Spfx = oT.ap().bitcast(F32)[:, 15360:16384]

    PF = [psf(f"PF{i}") for i in range(6)]
    PB = [psb(f"PB{i}") for i in range(2)]

    c_v = 0
    c_kt = 1024
    c_kr = 2048
    c_eb = 3072
    c_sb = 4096
    c_qt = 6144
    c_qr = 8192
    f_qraw = 5120
    f_sA = 5376
    f_sB = 5504
    f_sC = 5632
    f_sD = 5760
    f_B = 5888
    f_enB = 6016
    f_dec = 6144
    f_kr = 6272
    f_sc = 6528
    f_ep = 6656
    f_cols = 6784
    f_x2 = 6848
    c_ks = 13696
    c_kh = 13824
    c_qs = 13952
    c_pb = 14208
    c_Pb = 14336
    c_pT = 14464

    scol = lambda i: R2f[:, f_cols + i:f_cols + i + 1]

    def ln_tile(xsrc, tile, dst_ap):
        xb = R2f[:, 0:4096]
        k.wait("sq", "act")
        k.dma("sq", xb, xsrc[tile * 128:(tile + 1) * 128, :], "xq")
        k.wait("dve", "xq")
        k.wait("dve", "act")
        mcol = scol(0)
        k.fop("dve", lambda: V.reduce_sum(mcol, xb, axis=AX))
        k.fop("dve", lambda: V.tensor_scalar_mul(mcol, mcol, 1.0 / 4096.0))
        sq = R2f[:, f_x2:f_x2 + 512]
        ss = R2f[:, f_cols + 8:f_cols + 16]
        k.wait("act", "xq")
        k.wait("act", "dve")
        for w in range(8):
            k.op("act", lambda w=w: A_.activation(
                sq, xb[:, w * 512:(w + 1) * 512], AF.Square,
                accum_out=ss[:, w:w + 1]), mark="act" if w == 7 else None)
        k.wait("dve", "act")
        s2 = scol(1)
        k.fop("dve", lambda: V.reduce_sum(s2, ss, axis=AX))
        var = scol(2)
        k.fop("dve", lambda: V.tensor_scalar_mul(var, s2, 1.0 / 4096.0))
        m2 = scol(3)
        k.fop("dve", lambda: V.tensor_mul(m2, mcol, mcol))
        k.fop("dve", lambda: V.tensor_sub(var, var, m2))
        k.fop("dve", lambda: V.tensor_scalar_add(var, var, 1.0e-5))
        k.wait("act", "dve")
        sd = scol(4)
        k.op("act", lambda: A_.activation(sd, var, AF.Sqrt), mark="act")
        k.wait("dve", "act")
        rstd = scol(5)
        k.fop("dve", lambda: V.reciprocal(rstd, sd))
        nmr = scol(6)
        k.fop("dve", lambda: V.tensor_mul(nmr, mcol, rstd))
        k.fop("dve", lambda: V.tensor_scalar_mul(nmr, nmr, -1.0))
        k.wait("act", "dve")
        k.wait("act", "pe")
        k.op("act", lambda: A_.activation(dst_ap, xb, AF.Identity,
                                          bias=nmr, scale=rstd), mark="act")

    tp_war = {0: 0, 1: 0}

    def transpose32(src_bf16, dst_tensor, stride, off):
        k.wait("pe", "act")
        dstv = dst_tensor.ap().rearrange("p (a q) -> p a q", q=stride)
        for grp in range(8):
            bank = PB[grp % 2]
            if tp_war[grp % 2]:
                k.wait("pe", "dve", tp_war[grp % 2])
            for j in range(4):
                c = grp * 4 + j
                k.op("pe", lambda bank=bank, j=j, c=c, src=src_bf16: P.transpose(
                    bank[:, j * 128:(j + 1) * 128], src[:, c * 128:(c + 1) * 128], idb[:]),
                    mark="pe" if j == 3 else None)
            k.wait("dve", "pe")
            tp_war[grp % 2] = k.op(
                "dve", lambda bank=bank, grp=grp: V.tensor_copy(
                    dstv[:, grp * 4:grp * 4 + 4, off:off + 128],
                    bank[:, 0:512].rearrange("p (a b) -> p a b", b=128)),
                mark="dve")

    def rope6(src, dst, co, si, tmpa, tmpb):
        k.op("dve", lambda: V.tensor_tensor(tmpa[:, 0:64], src[:, 0:64], co, ALU.mult))
        k.op("dve", lambda: V.tensor_tensor(tmpa[:, 64:128], src[:, 64:128], si, ALU.mult))
        k.op("dve", lambda: V.tensor_tensor(tmpb[:, 0:64], src[:, 64:128], co, ALU.mult))
        k.fop("dve", lambda: V.tensor_tensor(tmpb[:, 64:128], src[:, 0:64], si, ALU.mult))
        k.op("dve", lambda: V.tensor_sub(dst[:, 0:64], tmpa[:, 0:64], tmpa[:, 64:128]))
        k.fop("dve", lambda: V.tensor_add(dst[:, 64:128], tmpb[:, 0:64], tmpb[:, 64:128]))

    kdk = {}

    def kraw_ap(r):
        return R2f[:, f_qraw + 128 * r:f_qraw + 128 * r + 128]

    def k_stage1(r, own, halo, t):
        kraw = kraw_ap(r)
        k.wait("dve", "act")
        k.fop("dve", lambda: V.reduce_max(scol(16), kraw, axis=AX, negate=True))
        k.wait("act", "dve")
        ek = R2f[:, f_sA:f_sA + 128]
        k.op("act", lambda: A_.activation(ek, kraw, AF.Exp, bias=scol(16),
                                          accum_out=scol(17)), mark="act")
        k.wait("dve", "act")
        k.fop("dve", lambda: V.reciprocal(scol(18), scol(17)))
        ks = (R2[:, c_ks:c_ks + 128] if r == 0 else R2[:, c_qs:c_qs + 128])
        k.fop("dve", lambda ks=ks: V.tensor_scalar(ks, ek, scol(18), None, ALU.mult))
        sB = R2f[:, f_sB:f_sB + 128]
        sC = R2f[:, f_sC:f_sC + 128]
        gsr = R2f[:, f_sD:f_sD + 128] if r == 0 else R2f[:, f_ep:f_ep + 128]
        k.fop("act", lambda: A_.activation(sB, kraw, AF.Abs))
        k.fop("act", lambda: A_.activation(sC, sB, AF.Exp, scale=-1.0))
        k.op("act", lambda: A_.activation(sB, sC, AF.Ln, bias=1.0), mark="act")
        k.wait("dve", "act")
        k.fop("dve", lambda: V.tensor_scalar_min(sC, kraw, 0.0))
        k.fop("dve", lambda gsr=gsr: V.tensor_sub(gsr, sC, sB))
        if own or halo:
            co = cosb[:, 512:576] if halo else cosb[:, t * 64:(t + 1) * 64]
            si = sinb[:, 512:576] if halo else sinb[:, t * 64:(t + 1) * 64]
            kr = R2f[:, f_kr:f_kr + 128] if r == 0 else R2f[:, f_sc:f_sc + 128]
            rope6(kraw, kr, co, si, sC, R2f[:, f_kr + 128:f_kr + 256])
        kdk[("s1", r)] = (k.cnt["dve"], k.cnt["act"])

    def k_rest(r, g, t, v_ap, chunk_base, own, halo):
        ks = (R2[:, c_ks:c_ks + 128] if r == 0 else R2[:, c_qs:c_qs + 128])
        gsr = R2f[:, f_sD:f_sD + 128] if r == 0 else R2f[:, f_ep:f_ep + 128]
        kr = R2f[:, f_kr:f_kr + 128] if r == 0 else R2f[:, f_sc:f_sc + 128]
        dv, ac = kdk[("s1", r)]
        k.wait("pe", "dve", max(dv, kdk.get("pf2rd", 0), kdk.get("pb0rd", 0)))
        k.op("pe", lambda gsr=gsr: P.transpose(PF[2][:, 0:128], gsr, idf[:]), mark="pe")
        if own:
            k.op("pe", lambda ks=ks: P.transpose(PB[0][:, 0:128], ks, idb[:]), mark="pe")
        if own or halo:
            k.op("pe", lambda kr=kr: P.transpose(PF[2][:, 256:384], kr, idf[:]), mark="pe")
        rt1 = k.cnt["pe"]
        k.wait("dve", "pe", rt1)
        gsT = R2f[:, f_kr + 128:f_kr + 256]
        Bc = R2f[:, f_B:f_B + 128]
        k.fop("dve", lambda: V.tensor_copy(gsT, PF[2][:, 0:128]))
        k.fop("dve", lambda: V.tensor_tensor_scan(
            Bc[:, 0:64], gsT[:, 0:64], zeros64[:], 0.0, ALU.add, ALU.add))
        k.fop("dve", lambda: V.tensor_tensor_scan(
            Bc[:, 64:128], gsT[:, 64:128], zeros64[:], 0.0, ALU.add, ALU.add))
        k.op("dve", lambda: V.tensor_scalar(scol(19), Bc[:, 63:64], G16, None, ALU.mult))
        k.fop("dve", lambda: V.tensor_scalar(scol(20), Bc[:, 127:128], G16, None, ALU.mult))
        if own or halo:
            if own:
                k.op("dve", lambda t=t: V.tensor_copy(
                    R2[:, c_kr + t * 128:c_kr + (t + 1) * 128], PF[2][:, 256:384]),
                    mark="dve")
            else:
                k.op("dve", lambda g=g: V.tensor_copy(
                    khalo[:, g * 64:(g + 1) * 64], PF[2][:, 320:384]), mark="dve")
        if halo:
            k.op("dve", lambda g=g, v_ap=v_ap: V.tensor_copy(
                vhalo[:, g * 128:(g + 1) * 128], v_ap), mark="dve")
        k.wait("act", "dve")
        enB = R2f[:, f_enB:f_enB + 128]
        dec = R2f[:, f_dec:f_dec + 128]
        if own:
            k.op("act", lambda t=t: A_.activation(
                R2[:, c_eb + t * 128:c_eb + (t + 1) * 128], Bc, AF.Exp, scale=G16))
            k.op("act", lambda: A_.activation(enB, Bc, AF.Exp, scale=-G16))
        k.op("act", lambda: A_.activation(dec[:, 0:64], Bc[:, 0:64], AF.Exp,
                                          bias=scol(19), scale=-G16))
        k.op("act", lambda: A_.activation(dec[:, 64:128], Bc[:, 64:128], AF.Exp,
                                          bias=scol(20), scale=-G16))
        k.op("act", lambda: A_.activation(scol(21), scol(19), AF.Exp))
        k.op("act", lambda: A_.activation(scol(22), scol(20), AF.Exp), mark="act")
        if own:
            k.wait("dve", "act")
            k.op("dve", lambda t=t: V.tensor_tensor(
                R2[:, c_kt + t * 128:c_kt + (t + 1) * 128], PB[0][:, 0:128], enB,
                ALU.mult), mark="dve")
            kdk["pb0rd"] = k.cnt["dve"]
        k.wait("pe", "act")
        k.wait("pe", "dve")
        k.op("pe", lambda: P.transpose(PF[2][:, 128:256], dec, idf[:]), mark="pe")
        rt2 = k.cnt["pe"]
        k.wait("dve", "pe", rt2)
        kh = R2[:, c_kh:c_kh + 128]
        k.op("dve", lambda ks=ks: V.tensor_tensor(kh, PF[2][:, 128:256], ks, ALU.mult),
             mark="dve")
        kdk["pf2rd"] = k.cnt["dve"]
        for cc in range(2):
            c = chunk_base + cc
            if own:
                k.op("dve", lambda c=c: V.tensor_copy(
                    R2[:, c_sb + c * 128:c_sb + (c + 1) * 128], Sf[:]), mark="dve")
            k.wait("pe", "dve")
            k.op("pe", lambda cc=cc, v_ap=v_ap: P.matmul(
                PF[3][:, 0:128], R2[64 * cc:64 * cc + 64, c_kh:c_kh + 128],
                v_ap[64 * cc:64 * cc + 64, :], start=True, stop=True), mark="pe")
            k.wait("dve", "pe")
            k.fop("dve", lambda cc=cc: V.scalar_tensor_tensor(
                Sf[:], Sf[:], scol(21 + cc), PF[3][:, 0:128], ALU.mult, ALU.add))

    k.dma("sq", cosb.ap().rearrange("p (t n) -> p t n", n=64)[:, 0:8, :],
          cosd.rearrange("(t p) n -> p t n", p=128), "tbl")
    k.dma("sq", sinb.ap().rearrange("p (t n) -> p t n", n=64)[:, 0:8, :],
          sind.rearrange("(t p) n -> p t n", p=128), "tbl")
    k.dma("sq", cosb[:, 512:576], cospd, "tbl")
    k.dma("sq", sinb[:, 512:576], sinpd, "tbl")
    k.dma("sq", trim[:], trimd, "tbl")
    k.dma("sq", swastd[:], swastdd, "tbl")
    k.dma("sq", swac0[:], swac0d, "tbl")
    k.dma("sq", idf[:], identd, "tbl")
    k.dma("gq", idb[:], identd, "gtbl")
    k.dma("sq", csw[:, 2:4], coefr, "tbl")
    k.dma("sq", csw[:, 4:6], scalr, "tbl")
    k.wait("dve", "tbl")
    k.wait("dve", "gtbl")
    k.fop("dve", lambda: V.tensor_mul(csw[:, 0:2], csw[:, 2:4], csw[:, 4:6]))
    k.fop("dve", lambda: V.memset(zeros64[:], 0.0))
    k.fop("dve", lambda: V.memset(Sf[:], 0.0))
    k.fop("dve", lambda: V.memset(oT[:], 0.0))
    k.barrier()

    for t in range(NT):
        xn_tok = R2[:, 8192:12288]
        ln_tile(x_own, t, xn_tok)
        transpose32(xn_tok, xnT, 1024, t * 128)
    k.barrier()
    if DEBUG:
        k.dma("sq", dbg_d[:, 0:32768], xnT[:], "oo")
        k.barrier()

    if ENABLE_ATTN:
        wslot0 = oT.ap()[:, 0:16384]
        for halfp in range(2):
            for th in range(4):
                t = halfp * 4 + th
                xn_tok = R2[:, 8192:12288]
                ln_tile(x_pfx, t, xn_tok)
                transpose32(xn_tok, ar1, 512, th * 128)
            k.barrier()
            evtokp = [0, 0]
            pend = []

            def p1_rest(item):
                r, g, t, v_ap = item
                k.fop("dve", lambda g=g: V.tensor_copy(Sf[:], Spfx[:, g * 128:(g + 1) * 128]))
                k_rest(r, g, t, v_ap, 0, False, t == 7)
                k.fop("dve", lambda g=g: V.tensor_copy(Spfx[:, g * 128:(g + 1) * 128], Sf[:]))

            for cb in range(4):
                wsrc = [Wv, Wv, Wk, Wk][cb]
                coff = [0, 512, 0, 512][cb]
                k.wait("gq", "pe")
                k.dma("gq", wslot0.rearrange("p (c n) -> p c n", n=512),
                      wsrc.rearrange("(c p) n -> p c n", p=128)[:, :, coff:coff + 512],
                      "wq")
                k.wait("pe", "wq")
                k.wait("pe", "dve")
                for th in range(4):
                    bank = PF[th % 2]
                    k.wait("pe", "dve", evtokp[th % 2])
                    for c in range(DCH):
                        k.op("pe", lambda bank=bank, c=c, th=th: P.matmul(
                            bank[:, 0:512],
                            ar1[:, c * 512 + th * 128:c * 512 + (th + 1) * 128],
                            wslot0.rearrange("p (c n) -> p c n", n=512)[:, c, :],
                            start=(c == 0), stop=(c == DCH - 1)),
                            mark="pe" if c == DCH - 1 else None)
                    pj = k.cnt["pe"]
                    k.wait("dve", "pe", pj)
                    if cb < 2:
                        evtokp[th % 2] = k.op("dve", lambda bank=bank, th=th, cb=cb: V.tensor_copy(
                            R2[:, th * 1024 + cb * 512:th * 1024 + cb * 512 + 512],
                            bank[:, 0:512]), mark="dve")
                    else:
                        t = halfp * 4 + th
                        for gg in range(4):
                            g = gg + (cb - 2) * 4
                            r = (th * 4 + gg) % 2
                            k.fop("dve", lambda bank=bank, gg=gg, r=r: V.tensor_copy(
                                kraw_ap(r), bank[:, gg * 128:(gg + 1) * 128]))
                            v_ap = R2[:, th * 1024 + g * 128:th * 1024 + (g + 1) * 128]
                            k_stage1(r, False, t == 7, t)
                            pend.append((r, g, t, v_ap))
                            if len(pend) > 1:
                                p1_rest(pend.pop(0))
                        evtokp[th % 2] = k.cnt["dve"]
            while pend:
                p1_rest(pend.pop(0))
            k.barrier()

        wring = [ar1.ap()[:, 0:8192], ar1.ap()[:, 8192:16384]]
        wtok = [0, 0]
        wsem = ["wr0", "wr1"]
        ring_i = [0]

        def wload(srcs):
            p = ring_i[0] % 2
            ring_i[0] += 1
            if wtok[p]:
                k.wait("gq", "pe", wtok[p])
            slot = wring[p]
            for src3, dcol, dw in srcs:
                k.dma("gq", slot.rearrange("p (c n) -> p c n", n=256)[:, :, dcol:dcol + dw],
                      src3, wsem[p])
            return p, slot

        Wq3 = Wq.rearrange("(c p) n -> p c n", p=128)
        Wk3 = Wk.rearrange("(c p) n -> p c n", p=128)
        Wv3 = Wv.rearrange("(c p) n -> p c n", p=128)

        def proj_tile(slot, t, width, bank):
            for c in range(DCH):
                k.op("pe", lambda bank=bank, c=c, t=t, slot=slot, width=width: P.matmul(
                    bank[:, 0:width],
                    xnT[:, c * 1024 + t * 128:c * 1024 + (t + 1) * 128],
                    slot.rearrange("p (c n) -> p c n", n=256)[:, c, 0:width],
                    start=(c == 0), stop=(c == DCH - 1)),
                    mark="pe" if c == DCH - 1 else None)

        pkv, skv = wload([(Wk3[:, :, 0:128], 0, 128), (Wv3[:, :, 0:128], 128, 128)])
        for g in range(KVH if ATTN_STAGE >= 2 else 0):
            k.fop("dve", lambda g=g: V.tensor_copy(Sf[:], Spfx[:, g * 128:(g + 1) * 128]))
            k.wait("pe", wsem[pkv])
            k.wait("pe", "dve")
            evtok = [0, 0]
            vaps = {}
            for t in range(NT):
                bank = PF[t % 2]
                k.wait("pe", "dve", evtok[t % 2])
                proj_tile(skv, t, 256, bank)
                pj = k.cnt["pe"]
                k.wait("dve", "pe", pj)
                r = t % 2
                k.fop("dve", lambda bank=bank, r=r: V.tensor_copy(
                    kraw_ap(r), bank[:, 0:128]))
                v_ap = R2[:, c_v + t * 128:c_v + (t + 1) * 128]
                vaps[t] = v_ap
                evtok[r] = k.op("dve", lambda bank=bank, v_ap=v_ap: V.tensor_copy(
                    v_ap, bank[:, 128:256]), mark="dve")
                k_stage1(r, True, False, t)
                if t >= 1:
                    k_rest((t - 1) % 2, g, t - 1, vaps[t - 1], 2 * (t - 1), True, False)
            k_rest(1, g, 7, vaps[7], 14, True, False)
            kv_done = k.cnt["pe"]
            wtok[pkv] = kv_done
            if ATTN_STAGE < 3:
                if g + 1 < KVH:
                    pkv, skv = wload([(Wk3[:, :, (g + 1) * 128:(g + 2) * 128], 0, 128),
                                      (Wv3[:, :, (g + 1) * 128:(g + 2) * 128], 128, 128)])
                continue
            pq0, sq0 = wload([(Wq3[:, :, g * 512:g * 512 + 256], 0, 256)])
            for hp in range(2 if ATTN_STAGE >= 3 else 0):
                psl, ssl = (pq0, sq0) if hp == 0 else wload(
                    [(Wq3[:, :, g * 512 + 256:g * 512 + 512], 0, 256)])
                k.wait("pe", wsem[psl])
                k.wait("pe", "dve")
                qd = {"pb1rd": k.cnt["dve"], "pf4rd": k.cnt["dve"]}
                qevt = [0, 0]
                qrW = R2.ap()[:, c_qr:c_qr + 2048].rearrange("p (cc q) -> p cc q", q=128)

                def q_stage1(t):
                    r = t % 2
                    qraw = (R2f[:, f_qraw:f_qraw + 256] if r == 0
                            else R2f[:, 6528:6784])
                    bank = PF[t % 2]
                    k.fop("dve", lambda bank=bank, qraw=qraw: V.tensor_copy(
                        qraw, bank[:, 0:256]))
                    qevt[r] = k.cnt["dve"]
                    k.op("dve", lambda qraw=qraw: V.reduce_max(
                        scol(24), qraw[:, 0:128], axis=AX, negate=True))
                    k.fop("dve", lambda qraw=qraw: V.reduce_max(
                        scol(25), qraw[:, 128:256], axis=AX, negate=True))
                    k.wait("act", "dve")
                    eq = R2f[:, f_B:f_B + 256]
                    k.op("act", lambda qraw=qraw, eq=eq: A_.activation(
                        eq[:, 0:128], qraw[:, 0:128], AF.Exp, bias=scol(24),
                        accum_out=scol(26)))
                    k.op("act", lambda qraw=qraw, eq=eq: A_.activation(
                        eq[:, 128:256], qraw[:, 128:256], AF.Exp, bias=scol(25),
                        accum_out=scol(27)), mark="act")
                    k.wait("dve", "act")
                    k.fop("dve", lambda: V.reciprocal(R2f[:, f_cols + 28:f_cols + 30],
                                                      R2f[:, f_cols + 26:f_cols + 28]))
                    k.fop("dve", lambda: V.tensor_scalar(
                        R2f[:, f_cols + 28:f_cols + 30],
                        R2f[:, f_cols + 28:f_cols + 30], 0.5, None, ALU.mult))
                    qs = (R2[:, c_qs:c_qs + 256] if r == 0 else R2[:, c_ks:c_ks + 256])
                    k.op("dve", lambda eq=eq, qs=qs: V.tensor_scalar(
                        qs[:, 0:128], eq[:, 0:128], scol(28), None, ALU.mult))
                    qd[("qs", t)] = k.op("dve", lambda eq=eq, qs=qs: V.tensor_scalar(
                        qs[:, 128:256], eq[:, 128:256], scol(29), None, ALU.mult),
                        mark="dve")
                    co = cosb[:, t * 64:(t + 1) * 64]
                    si = sinb[:, t * 64:(t + 1) * 64]
                    kr = (R2f[:, f_kr:f_kr + 256] if r == 0
                          else R2f[:, f_sA:f_sA + 256])
                    for h2 in range(2):
                        rope6(qraw[:, h2 * 128:(h2 + 1) * 128],
                              kr[:, h2 * 128:(h2 + 1) * 128], co, si,
                              R2f[:, f_sC:f_sC + 128], R2f[:, f_sD:f_sD + 128])
                    qd[("rope", t)] = k.cnt["dve"]

                def q_rest(t):
                    r = t % 2
                    qs = (R2[:, c_qs:c_qs + 256] if r == 0 else R2[:, c_ks:c_ks + 256])
                    kr = (R2f[:, f_kr:f_kr + 256] if r == 0
                          else R2f[:, f_sA:f_sA + 256])
                    k.wait("pe", "dve", max(qd[("qs", t)], qd["pb1rd"]))
                    for h2 in range(2):
                        k.op("pe", lambda h2=h2, qs=qs: P.transpose(
                            PB[1][:, h2 * 128:(h2 + 1) * 128],
                            qs[:, h2 * 128:(h2 + 1) * 128], idb[:]),
                            mark="pe" if h2 == 1 else None)
                    qtp = k.cnt["pe"]
                    k.wait("pe", "dve", max(qd[("rope", t)], qd["pf4rd"]))
                    for h2 in range(2):
                        k.op("pe", lambda h2=h2, kr=kr: P.transpose(
                            PF[4][:, h2 * 128:(h2 + 1) * 128],
                            kr[:, h2 * 128:(h2 + 1) * 128], idf[:]),
                            mark="pe" if h2 == 1 else None)
                    qrp = k.cnt["pe"]
                    k.wait("dve", "pe", qtp)
                    for h2 in range(2):
                        k.op("dve", lambda h2=h2, t=t: V.tensor_tensor(
                            R2[:, c_qt + h2 * 1024 + t * 128:c_qt + h2 * 1024 + (t + 1) * 128],
                            PB[1][:, h2 * 128:(h2 + 1) * 128],
                            R2[:, c_eb + t * 128:c_eb + (t + 1) * 128], ALU.mult),
                            mark="dve" if h2 == 1 else None)
                    qd["pb1rd"] = k.cnt["dve"]
                    k.wait("dve", "pe", qrp)
                    for h2 in range(2):
                        k.op("dve", lambda h2=h2, t=t: V.tensor_copy(
                            qrW[:, 2 * t:2 * t + 2, h2 * 64:(h2 + 1) * 64],
                            PF[4][:, h2 * 128:(h2 + 1) * 128].rearrange(
                                "p (a u) -> p a u", u=64)),
                            mark="dve" if h2 == 1 else None)
                    qd["pf4rd"] = k.cnt["dve"]

                for t in range(NT):
                    bank = PF[t % 2]
                    k.wait("pe", "dve", qevt[t % 2])
                    proj_tile(ssl, t, 256, bank)
                    pj = k.cnt["pe"]
                    k.wait("dve", "pe", pj)
                    q_stage1(t)
                    if t >= 1:
                        q_rest(t - 1)
                q_rest(7)
                wtok[psl] = k.cnt["pe"]
                if hp == 1 and g + 1 < KVH:
                    pkv, skv = wload([(Wk3[:, :, (g + 1) * 128:(g + 2) * 128], 0, 128),
                                      (Wv3[:, :, (g + 1) * 128:(g + 2) * 128], 128, 128)])
                qtv = R2.ap()[:, c_qt:c_qt + 2048].rearrange("p (a q) -> p a q", q=1024)
                qt3 = lambda c: qtv[:, :, c * 64:(c + 1) * 64]
                otv = oT.ap().rearrange("p (a q) -> p a q", q=1024)
                oev = {0: 0, 1: 0}
                tok = {}
                pbv = [R2[:, c_pb:c_pb + 128], R2[:, c_qs:c_qs + 128]]
                pTa = R2[:, c_pT:c_pT + 128]
                pTb = R2[:, c_pT + 128:c_pT + 256]
                Pb = R2[:, c_Pb:c_Pb + 128]

                qr3 = lambda c: R2[:, c_qr + c * 128:c_qr + (c + 1) * 128]

                def em_scores(c):
                    k.wait("pe", "dve", tok.get(("stt", c - 2), 0))
                    if c == 0:
                        k.op("pe", lambda g=g: P.matmul(
                            PF[0][:, 0:64], qr3(0),
                            khalo[:, g * 64:(g + 1) * 64], start=True, stop=True))
                        k.op("pe", lambda: P.matmul(
                            PF[0][:, 64:128], qr3(0),
                            R2[:, c_kr:c_kr + 64], start=True, stop=True), mark="pe")
                    else:
                        k.op("pe", lambda c=c: P.matmul(
                            PF[c % 2][:, 0:128], qr3(c),
                            R2[:, c_kr + (c - 1) * 64:c_kr + (c + 1) * 64],
                            start=True, stop=True), mark="pe")
                    tok[("scpe", c)] = k.cnt["pe"]

                def em_softmax(c):
                    k.wait("dve", "pe", tok[("scpe", c)])
                    if ("exp", c - 1) in tok:
                        k.wait("dve", "act", tok[("exp", c - 1)])
                    msk = swac0 if c == 0 else swastd
                    sc_ = R2f[:, f_sc:f_sc + 128]
                    k.fop("dve", lambda sc_=sc_, msk=msk, c=c: V.scalar_tensor_tensor(
                        sc_, PF[c % 2][:, 0:128], RSQ, msk[:], ALU.mult, ALU.add))
                    tok[("stt", c)] = k.cnt["dve"]
                    k.fop("dve", lambda sc_=sc_: V.reduce_max(
                        scol(32), sc_, axis=AX, negate=True))
                    k.wait("act", "dve")
                    ep = R2f[:, f_ep:f_ep + 128]
                    tok[("exp", c)] = k.op("act", lambda ep=ep, sc_=sc_: A_.activation(
                        ep, sc_, AF.Exp, bias=scol(32), accum_out=scol(33)), mark="act")
                    k.wait("dve", "act", tok[("exp", c)])
                    k.fop("dve", lambda: V.reciprocal(scol(34), scol(33)))
                    k.fop("dve", lambda: V.tensor_scalar(scol(34), scol(34), 0.5, None, ALU.mult))
                    tok[("p", c)] = k.op("dve", lambda c=c, ep=ep: V.tensor_scalar(
                        pbv[c % 2], ep, scol(34), None, ALU.mult), mark="dve")

                k.wait("pe", "dve")
                em_scores(0)
                em_softmax(0)
                for c in range(16):
                    t = c // 2
                    cp = c % 2
                    if c + 1 < 16:
                        em_scores(c + 1)
                    k.wait("pe", "dve", max(tok[("p", c)], tok.get(("ptc", c - 1), 0)))
                    pb = pbv[c % 2]
                    if cp == 1:
                        k.op("pe", lambda pb=pb: P.transpose(PB[0][:, 0:128], pb, idb[:]),
                             mark="pe")
                    else:
                        k.op("pe", lambda pb=pb: P.transpose(
                            PB[0][64:128, 0:128], pb[:, 0:64], idb[:]))
                        k.op("pe", lambda pb=pb: P.transpose(
                            PB[0][0:64, 0:128], pb[:, 64:128], idb[:]), mark="pe")
                    tok[("ptpe", c)] = k.cnt["pe"]
                    k.wait("pe", "dve", tok.get(("mask", c - 1), 0))
                    klo = (c - 1) * 64 if cp == 1 else c * 64
                    k.op("pe", lambda c=c, klo=klo: P.matmul(
                        PF[4][:, 0:128],
                        R2[:, c_kt + klo:c_kt + klo + 128], qt3(c),
                        start=True, stop=True), mark="pe")
                    tok[("phpe", c)] = k.cnt["pe"]
                    k.wait("dve", "pe", tok[("ptpe", c)])
                    if cp == 1:
                        k.op("dve", lambda: V.tensor_copy(pTa[:], PB[0][:, 0:128]),
                             mark="dve")
                    else:
                        k.op("dve", lambda: V.memset(pTa[0:64, :], 0.0))
                        k.op("dve", lambda: V.tensor_copy(pTa[64:128, :], PB[0][64:128, 0:128]))
                        k.op("dve", lambda: V.tensor_copy(pTb[0:64, :], PB[0][0:64, 0:128]))
                        k.op("dve", lambda: V.memset(pTb[64:128, :], 0.0), mark="dve")
                    tok[("ptc", c)] = k.cnt["dve"]
                    k.wait("dve", "pe", tok[("phpe", c)])
                    trv = trim[:, 0:128] if cp == 0 else trim[:, 128:256]
                    tok[("mask", c)] = k.op("dve", lambda trv=trv: V.tensor_tensor(
                        Pb[:], PF[4][:, 0:128], trv, ALU.mult), mark="dve")
                    if c + 1 < 16:
                        em_softmax(c + 1)
                    ob = PF[2 + (c % 2)]
                    k.wait("pe", "dve", max(tok[("mask", c)], oev[c % 2]))
                    vt = R2[:, c_v + t * 128:c_v + (t + 1) * 128]
                    k.op("pe", lambda ob=ob, vt=vt: P.matmul(
                        ob[:, 0:128], vt, Pb[:], start=True, stop=False))
                    k.op("pe", lambda ob=ob, c=c: P.matmul(
                        ob[:, 0:128], R2[:, c_sb + c * 128:c_sb + (c + 1) * 128], qt3(c),
                        start=False, stop=False))
                    if cp == 1:
                        k.op("pe", lambda ob=ob, vt=vt: P.matmul(
                            ob[:, 0:128], vt, pTa[:], start=False, stop=True), mark="pe")
                    else:
                        vprev = (vhalo[:, g * 128:(g + 1) * 128] if c == 0 else
                                 R2[:, c_v + (t - 1) * 128:c_v + t * 128])
                        k.op("pe", lambda ob=ob, vprev=vprev: P.matmul(
                            ob[:, 0:128], vprev, pTa[:], start=False, stop=False))
                        k.op("pe", lambda ob=ob, vt=vt: P.matmul(
                            ob[:, 0:128], vt, pTb[:], start=False, stop=True), mark="pe")
                    tok[("ope", c)] = k.cnt["pe"]
                    k.wait("dve", "pe", tok[("ope", c)])
                    oev[c % 2] = k.op("dve", lambda ob=ob, g=g, hp=hp, c=c: V.tensor_copy(
                        otv[:, 4 * g + 2 * hp:4 * g + 2 * hp + 2, c * 64:(c + 1) * 64],
                        ob[:, 0:128].rearrange("p (a b) -> p a b", b=64)), mark="dve")
        k.barrier()

    k.dma("gq", ar1.ap()[:, 0:8192].rearrange("p (c n) -> p c n", n=256),
          Aft.rearrange("(c p) n -> p c n", p=128), "wr0")
    c_xa = 12288
    k.wait("pe", "wr0")
    k.wait("pe", "dve")
    for e2 in range(2):
        for th in range(2):
            bank = PF[(e2 * 2 + th) % 2]
            k.wait("pe", "dve")
            for c in range(DCH):
                k.op("pe", lambda bank=bank, c=c, e2=e2, th=th: P.matmul(
                    bank[:, 0:512],
                    ar1[:, c * 256 + e2 * 128:c * 256 + e2 * 128 + 128],
                    xnT[:, c * 1024 + th * 512:c * 1024 + (th + 1) * 512],
                    start=(c == 0), stop=(c == DCH - 1)),
                    mark="pe" if c == DCH - 1 else None)
            k.wait("dve", "pe")
            k.op("dve", lambda bank=bank, e2=e2, th=th: V.tensor_scalar(
                R2[:, c_xa + e2 * 1024 + th * 512:c_xa + e2 * 1024 + (th + 1) * 512],
                bank[:, 0:512], csw[:, e2:e2 + 1], None, ALU.mult), mark="dve")
    k.barrier()

    wslotA = [ar1.ap()[:, 0:8192], ar1.ap()[:, 8192:16384]]
    bfslot = [smallb.ap()[:, 0:512], smallb.ap()[:, 512:1024]]
    bslot = [R2.ap()[:, 2048:2304], R2.ap()[:, 2304:2560]]
    wtokA = [0, 0]
    xring = [R2f[:, 0:256], R2f[:, 256:512]]
    ering = [R2f[:, 512:768], R2f[:, 768:1024]]
    xrtok = [0, 0]
    WT3 = WT.rearrange("(c p) n -> p c n", p=128)
    Bf3 = Bft.rearrange("(e p) n -> p e n", p=128)

    def load_wt(j):
        p = j % 2
        if wtokA[p]:
            k.wait("gq", "pe", wtokA[p])
        k.dma("gq", wslotA[p].rearrange("p (c n) -> p c n", n=256),
              WT3[:, :, j * 256:(j + 1) * 256], f"wr{p}")
        k.dma("gq", bfslot[p].rearrange("p (e n) -> p e n", n=256),
              Bf3[:, :, j * 256:(j + 1) * 256], f"wr{p}")
        k.dma("gq", bslot[p], biasbc[:, j * 256:(j + 1) * 256], f"wr{p}")

    load_wt(0)
    if DEBUG:
        k.wait("sq", "wr0")
        k.dma("sq", dbg_d[:, 32768:40960], wslotA[0], "oo")
        k.barrier()
    for j in range(16):
        p = j % 2
        if j + 1 < 16:
            load_wt(j + 1)
        k.wait("pe", f"wr{p}")
        for t in range(NT):
            bank = PF[2 + (t % 2)]
            if xrtok[t % 2]:
                k.wait("pe", "dve", xrtok[t % 2])
            for c in range(DCH):
                k.op("pe", lambda bank=bank, c=c, p=p, t=t: P.matmul(
                    bank[:, 0:256],
                    xnT[:, c * 1024 + t * 128:c * 1024 + (t + 1) * 128],
                    wslotA[p].rearrange("p (c n) -> p c n", n=256)[:, c, :],
                    start=(c == 0), stop=False))
            for e2 in range(2):
                k.op("pe", lambda bank=bank, e2=e2, p=p, t=t: P.matmul(
                    bank[:, 0:256],
                    R2[:, c_xa + e2 * 1024 + t * 128:c_xa + e2 * 1024 + (t + 1) * 128],
                    bfslot[p].rearrange("p (e n) -> p e n", n=256)[:, e2, :],
                    start=False, stop=(e2 == 1)),
                    mark="pe" if e2 == 1 else None)
            wtokA[p] = k.cnt["pe"]
            xp = t % 2
            if xrtok[xp]:
                k.wait("sq", "dve", xrtok[xp])
            k.dma("sq", xring[xp], x_own[t * 128:(t + 1) * 128, j * 256:(j + 1) * 256], "xq")
            k.wait("dve", "pe")
            k.wait("dve", "xq")
            k.fop("dve", lambda bank=bank, xp=xp: V.scalar_tensor_tensor(
                ering[xp], bank[:, 0:256], 1.0, xring[xp], ALU.mult, ALU.add))
            xrtok[xp] = k.op("dve", lambda xp=xp, p=p: V.tensor_tensor(
                ering[xp], ering[xp], bslot[p], ALU.add), mark="dve")
            k.wait("sq", "dve", xrtok[xp])
            k.dma("sq", scr_d[t * 128:(t + 1) * 128, j * 256:(j + 1) * 256], ering[xp], "so")
    k.barrier()

    woslot = [xnT.ap()[:, 0:16384], xnT.ap()[:, 16384:32768]]
    wotok = [0, 0]
    sring = [R2f[:, 0:512], R2f[:, 512:1024]]
    oring = [R2f[:, 2048:2560], R2f[:, 2560:3072]]
    srtok = [0, 0]
    Wo3 = Wo.rearrange("(c p) n -> p c n", p=128)

    def load_wo(j):
        p = j % 2
        if wotok[p]:
            k.wait("gq", "pe", wotok[p])
        k.dma("gq", woslot[p].rearrange("p (c n) -> p c n", n=512),
              Wo3[:, :, j * 512:(j + 1) * 512], f"wr{p}")

    load_wo(0)
    for j in range(8):
        p = j % 2
        if j + 1 < 8:
            load_wo(j + 1)
        k.wait("pe", f"wr{p}")
        for t in range(NT):
            bank = PF[2 + (t % 2)]
            if srtok[t % 2]:
                k.wait("pe", "dve", srtok[t % 2])
            for h in range(NH):
                k.op("pe", lambda bank=bank, h=h, p=p, t=t: P.matmul(
                    bank[:, 0:512],
                    oT[:, h * 1024 + t * 128:h * 1024 + (t + 1) * 128],
                    woslot[p].rearrange("p (c n) -> p c n", n=512)[:, h, :],
                    start=(h == 0), stop=(h == NH - 1)),
                    mark="pe" if h == NH - 1 else None)
            wotok[p] = k.cnt["pe"]
            rp = t % 2
            if srtok[rp]:
                k.wait("sq", "dve", srtok[rp])
            k.dma("sq", sring[rp], scr_d[t * 128:(t + 1) * 128, j * 512:(j + 1) * 512], "si")
            k.wait("dve", "pe")
            k.wait("dve", "si")
            srtok[rp] = k.op("dve", lambda bank=bank, rp=rp: V.scalar_tensor_tensor(
                oring[rp], bank[:, 0:512], 1.0, sring[rp], ALU.mult, ALU.add), mark="dve")
            k.wait("sq", "dve", srtok[rp])
            k.dma("sq", out_d[t * 128:(t + 1) * 128, j * 512:(j + 1) * 512], oring[rp], "oo")

    k.wait("sq", "oo")
    k.emit()
    return nc


_CACHE = {}


def kernel(**inputs):
    nc = _CACHE.get("nc")
    if nc is None:
        nc = build_program()
        _CACHE["nc"] = nc

    x = np.ascontiguousarray(np.asarray(inputs["L0_attn_hidden_states"], np.float32))
    Wq = np.ascontiguousarray(np.asarray(inputs["L0_attn_Wq"], np.float32))
    Wk = np.ascontiguousarray(np.asarray(inputs["L0_attn_Wk"], np.float32))
    Wv = np.ascontiguousarray(np.asarray(inputs["L0_attn_Wv"], np.float32))
    Wo = np.ascontiguousarray(np.asarray(inputs["L0_attn_Wo"], np.float32))
    WT = np.ascontiguousarray(np.asarray(inputs["L0_ffn_W"], np.float32).T)
    A = np.asarray(inputs["L0_ffn_A"], np.float32)
    Bf = np.asarray(inputs["L0_ffn_Bf"], np.float32)
    bias = np.asarray(inputs["L0_ffn_bias"], np.float32)
    coeffs = np.asarray(inputs["L0_ffn_coeffs"], np.float32)
    scales = np.asarray(inputs["L0_ffn_scales"], np.float32)

    Aft = np.ascontiguousarray(A.transpose(2, 0, 1).reshape(4096, 256))
    Bft = np.ascontiguousarray(Bf.transpose(0, 2, 1).reshape(256, 4096))
    biasbc = np.ascontiguousarray(np.tile(bias[None, :], (128, 1)))
    coefr = np.ascontiguousarray(np.repeat(coeffs, 16).reshape(2, 128).T)
    scalr = np.ascontiguousarray(np.repeat(scales, 16).reshape(2, 128).T)

    inv = (1.0 / (10000.0 ** (2.0 * np.arange(64, dtype=np.float64) / 128.0)))
    a = np.arange(64)[:, None]
    cc = np.arange(128)[None, :]
    rel = a + 64 - cc
    basem = (rel >= 0) & (rel <= 64)
    swastd = np.where(basem, 0.0, NEG).astype(np.float32)
    swastd = np.concatenate([swastd, swastd], axis=0)
    b0 = basem & (cc >= 64)
    swa_even = np.concatenate([np.where(b0, 0.0, NEG).astype(np.float32)] * 2, axis=0)
    tk = np.arange(64)[:, None]
    tq = np.arange(64)[None, :]
    tri = (tq >= tk).astype(np.float32)
    base = np.concatenate([tri, tri], axis=1)
    z64 = np.zeros((64, 128), np.float32)
    trimm = np.concatenate([np.concatenate([base, z64], axis=0),
                            np.concatenate([z64, base], axis=0)], axis=1)
    ident = np.eye(128, dtype=np.float32)
    zeros_pfx = np.zeros((1024, 4096), np.float32)
    zeros_tab = np.zeros((128, 64), np.float32)

    in_maps = []
    for c in range(8):
        b, half = c // 2, c % 2
        pos = (half * 1024 + np.arange(1024, dtype=np.float64))[:, None] * inv[None, :]
        cos = np.cos(pos).astype(np.float32)
        sin = np.sin(pos).astype(np.float32)
        if half == 1:
            ppos = (960 + np.arange(64, dtype=np.float64))[:, None] * inv[None, :]
            pc = np.concatenate([zeros_tab[0:64], np.cos(ppos).astype(np.float32)], axis=0)
            psn = np.concatenate([zeros_tab[0:64], np.sin(ppos).astype(np.float32)], axis=0)
        else:
            pc, psn = zeros_tab, zeros_tab
        xo = np.ascontiguousarray(x[b, half * 1024:(half + 1) * 1024, :])
        xp = np.ascontiguousarray(x[b, 0:1024, :]) if half == 1 else zeros_pfx
        in_maps.append({
            "x_own": xo, "x_pfx": xp,
            "Wq": Wq, "Wk": Wk, "Wv": Wv, "Wo": Wo, "WT": WT,
            "Aft": Aft, "Bft": Bft, "biasbc": biasbc,
            "coefr": coefr, "scalr": scalr,
            "cosd": np.ascontiguousarray(cos), "sind": np.ascontiguousarray(sin),
            "cospd": np.ascontiguousarray(pc), "sinpd": np.ascontiguousarray(psn),
            "trimd": trimm,
            "swastdd": swastd,
            "swac0d": swa_even if half == 0 else swastd,
            "identd": ident,
        })

    res = run_bass_kernel_spmd(nc, in_maps, core_ids=list(range(8)))
    if DEBUG:
        _CACHE["res"] = res
    out = np.empty((4, 2048, 4096), np.float32)
    for c in range(8):
        b, half = c // 2, c % 2
        out[b, half * 1024:(half + 1) * 1024, :] = res.results[c]["out"]
    return out


# version 09 (final)
# speedup vs baseline: 1.1911x; 1.0073x over previous
import numpy as np
import concourse.bass as bass
import concourse.mybir as mybir
from concourse.bass_utils import run_bass_kernel_spmd
from contextlib import ExitStack

F32 = mybir.dt.float32
BF16 = mybir.dt.bfloat16
AF = mybir.ActivationFunctionType
ALU = mybir.AluOpType
AX = mybir.AxisListType.X

NT = 8
KVH = 8
NH = 32
DCH = 32
RSQ = float(1.0 / np.sqrt(np.float32(128.0)))
NEG = -1.0e30
G16 = 1.0 / 16.0

ENABLE_ATTN = True
ATTN_STAGE = 4
DEBUG = 0

ENGS = ("pe", "dve", "act", "sq", "gq")


class KB:
    def __init__(self, nc, ctx):
        self.nc = nc
        self.ctx = ctx
        self.ops = {e: [] for e in ENGS}
        self.sems = {}
        self.cnt = {}

    def sem(self, name):
        if name not in self.sems:
            self.sems[name] = self.ctx.enter_context(self.nc.semaphore(name))
            self.cnt[name] = 0
        return self.sems[name]

    def wait(self, e, name, val=None):
        s = self.sem(name)
        v = self.cnt[name] if val is None else val
        if v <= 0:
            return
        self.ops[e].append(lambda eng, s=s, v=v: eng.wait_ge(s, v))

    def op(self, e, fn, mark=None):
        if mark is None:
            self.ops[e].append(lambda eng, fn=fn: fn())
            return None
        s = self.sem(mark)
        self.cnt[mark] += 1
        self.ops[e].append(lambda eng, fn=fn, s=s: fn().then_inc(s, 1))
        return self.cnt[mark]

    def fop(self, e, fn):
        v = self.op(e, fn, mark=e)
        self.wait(e, e, v)
        return v

    def dma(self, e, out, in_, mark):
        s = self.sem(mark)
        self.cnt[mark] += 16
        self.ops[e].append(
            lambda eng, out=out, in_=in_, s=s: eng.dma_start(out=out, in_=in_).then_inc(s, 16))
        return self.cnt[mark]

    def barrier(self):
        snap = [(n, self.cnt[n]) for n in list(self.cnt) if self.cnt[n] > 0]
        for e in ENGS:
            for n, v in snap:
                self.wait(e, n, v)

    def emit(self):
        with self.nc.Block() as block:
            decs = {"sq": block.sync, "gq": block.gpsimd, "pe": block.tensor,
                    "dve": block.vector, "act": block.scalar}
            for e in ENGS:
                def body(eng, e=e):
                    for f in self.ops[e]:
                        f(eng)
                decs[e](body)


def build_program():
    nc = bass.Bass("TRN2", target_bir_lowering=False)
    ctx = ExitStack()
    k = KB(nc, ctx)
    V, A_, P = nc.vector, nc.scalar, nc.tensor

    di = lambda n, s: nc.dram_tensor(n, s, F32, kind="ExternalInput").ap()
    x_own = di("x_own", [1024, 4096])
    x_pfx = di("x_pfx", [1024, 4096])
    Wq = di("Wq", [4096, 4096])
    Wk = di("Wk", [4096, 1024])
    Wv = di("Wv", [4096, 1024])
    Wo = di("Wo", [4096, 4096])
    WT = di("WT", [4096, 4096])
    Aft = di("Aft", [4096, 256])
    Bft = di("Bft", [256, 4096])
    biasbc = di("biasbc", [128, 4096])
    coefr = di("coefr", [128, 2])
    scalr = di("scalr", [128, 2])
    cosd = di("cosd", [1024, 64])
    sind = di("sind", [1024, 64])
    cospd = di("cospd", [128, 64])
    sinpd = di("sinpd", [128, 64])
    trimd = di("trimd", [128, 256])
    swastdd = di("swastdd", [128, 128])
    swac0d = di("swac0d", [128, 128])
    identd = di("identd", [128, 128])

    out_d = nc.dram_tensor("out", [1024, 4096], F32, kind="ExternalOutput").ap()
    scr_d = nc.dram_tensor("scr", [1024, 4096], F32, kind="Internal").ap()
    dbg_d = (nc.dram_tensor("dbg", [128, 40960], BF16, kind="ExternalOutput").ap()
             if DEBUG else None)

    sb = lambda n, s, d: ctx.enter_context(nc.sbuf_tensor(n, s, d))
    psf = lambda n: ctx.enter_context(nc.psum_tensor(n, [128, 512], F32))
    psb = lambda n: ctx.enter_context(nc.psum_tensor(n, [128, 1024], BF16))

    xnT = sb("xnT", [128, 32768], BF16)
    oT = sb("oT", [128, 32768], BF16)
    ar1 = sb("ar1", [128, 16384], BF16)
    R2 = sb("R2", [128, 14720], BF16)
    R2f = R2.ap().bitcast(F32)

    cosb = sb("cosb", [128, 576], F32)
    sinb = sb("sinb", [128, 576], F32)
    trim = sb("trim", [128, 256], F32)
    swastd = sb("swastd", [128, 128], F32)
    swac0 = sb("swac0", [128, 128], F32)
    idf = sb("idf", [128, 128], F32)
    idb = sb("idb", [128, 128], BF16)
    Sf = sb("Sf", [128, 128], F32)
    csw = sb("csw", [128, 8], F32)
    zeros64 = sb("zeros64", [128, 64], F32)
    khalo = sb("khalo", [128, 512], BF16)
    vhalo = sb("vhalo", [128, 1024], BF16)
    smallb = sb("smallb", [128, 1024], BF16)

    Spfx = oT.ap().bitcast(F32)[:, 15360:16384]

    PF = [psf(f"PF{i}") for i in range(6)]
    PB = [psb(f"PB{i}") for i in range(2)]

    c_v = 0
    c_kt = 1024
    c_kr = 2048
    c_eb = 3072
    c_sb = 4096
    c_qt = 6144
    c_qr = 8192
    f_qraw = 5120
    f_sA = 5376
    f_sB = 5504
    f_sC = 5632
    f_sD = 5760
    f_B = 5888
    f_enB = 6016
    f_dec = 6144
    f_kr = 6272
    f_sc = 6528
    f_ep = 6656
    f_cols = 6784
    f_x2 = 6848
    c_ks = 13696
    c_kh = 13824
    c_qs = 13952
    c_pb = 14208
    c_Pb = 14336
    c_pT = 14464

    scol = lambda i: R2f[:, f_cols + i:f_cols + i + 1]

    def ln_tile(xsrc, tile, dst_ap):
        xb = R2f[:, 0:4096]
        k.wait("sq", "act")
        k.dma("sq", xb, xsrc[tile * 128:(tile + 1) * 128, :], "xq")
        k.wait("dve", "xq")
        k.wait("dve", "act")
        mcol = scol(0)
        k.fop("dve", lambda: V.reduce_sum(mcol, xb, axis=AX))
        k.fop("dve", lambda: V.tensor_scalar_mul(mcol, mcol, 1.0 / 4096.0))
        sq = R2f[:, f_x2:f_x2 + 512]
        ss = R2f[:, f_cols + 8:f_cols + 16]
        k.wait("act", "xq")
        k.wait("act", "dve")
        for w in range(8):
            k.op("act", lambda w=w: A_.activation(
                sq, xb[:, w * 512:(w + 1) * 512], AF.Square,
                accum_out=ss[:, w:w + 1]), mark="act" if w == 7 else None)
        k.wait("dve", "act")
        s2 = scol(1)
        k.fop("dve", lambda: V.reduce_sum(s2, ss, axis=AX))
        var = scol(2)
        k.fop("dve", lambda: V.tensor_scalar_mul(var, s2, 1.0 / 4096.0))
        m2 = scol(3)
        k.fop("dve", lambda: V.tensor_mul(m2, mcol, mcol))
        k.fop("dve", lambda: V.tensor_sub(var, var, m2))
        k.fop("dve", lambda: V.tensor_scalar_add(var, var, 1.0e-5))
        k.wait("act", "dve")
        sd = scol(4)
        k.op("act", lambda: A_.activation(sd, var, AF.Sqrt), mark="act")
        k.wait("dve", "act")
        rstd = scol(5)
        k.fop("dve", lambda: V.reciprocal(rstd, sd))
        nmr = scol(6)
        k.fop("dve", lambda: V.tensor_mul(nmr, mcol, rstd))
        k.fop("dve", lambda: V.tensor_scalar_mul(nmr, nmr, -1.0))
        k.wait("act", "dve")
        k.wait("act", "pe")
        k.op("act", lambda: A_.activation(dst_ap, xb, AF.Identity,
                                          bias=nmr, scale=rstd), mark="act")

    tp_war = {0: 0, 1: 0}

    def transpose32(src_bf16, dst_tensor, stride, off):
        k.wait("pe", "act")
        dstv = dst_tensor.ap().rearrange("p (a q) -> p a q", q=stride)
        for grp in range(8):
            bank = PB[grp % 2]
            if tp_war[grp % 2]:
                k.wait("pe", "dve", tp_war[grp % 2])
            for j in range(4):
                c = grp * 4 + j
                k.op("pe", lambda bank=bank, j=j, c=c, src=src_bf16: P.transpose(
                    bank[:, j * 128:(j + 1) * 128], src[:, c * 128:(c + 1) * 128], idb[:]),
                    mark="pe" if j == 3 else None)
            k.wait("dve", "pe")
            tp_war[grp % 2] = k.op(
                "dve", lambda bank=bank, grp=grp: V.tensor_copy(
                    dstv[:, grp * 4:grp * 4 + 4, off:off + 128],
                    bank[:, 0:512].rearrange("p (a b) -> p a b", b=128)),
                mark="dve")

    def rope6(src, dst, co, si, tmpa, tmpb):
        k.op("dve", lambda: V.tensor_tensor(tmpa[:, 0:64], src[:, 0:64], co, ALU.mult))
        k.op("dve", lambda: V.tensor_tensor(tmpa[:, 64:128], src[:, 64:128], si, ALU.mult))
        k.op("dve", lambda: V.tensor_tensor(tmpb[:, 0:64], src[:, 64:128], co, ALU.mult))
        k.fop("dve", lambda: V.tensor_tensor(tmpb[:, 64:128], src[:, 0:64], si, ALU.mult))
        k.op("dve", lambda: V.tensor_sub(dst[:, 0:64], tmpa[:, 0:64], tmpa[:, 64:128]))
        k.fop("dve", lambda: V.tensor_add(dst[:, 64:128], tmpb[:, 0:64], tmpb[:, 64:128]))

    kdk = {}

    def kraw_ap(r):
        return R2f[:, f_qraw + 128 * r:f_qraw + 128 * r + 128]

    def k_stage1(r, own, halo, t):
        kraw = kraw_ap(r)
        k.wait("dve", "act")
        k.fop("dve", lambda: V.reduce_max(scol(16), kraw, axis=AX, negate=True))
        k.wait("act", "dve")
        ek = R2f[:, f_sA:f_sA + 128]
        k.op("act", lambda: A_.activation(ek, kraw, AF.Exp, bias=scol(16),
                                          accum_out=scol(17)), mark="act")
        k.wait("dve", "act")
        k.fop("dve", lambda: V.reciprocal(scol(18), scol(17)))
        ks = (R2[:, c_ks:c_ks + 128] if r == 0 else R2[:, c_qs:c_qs + 128])
        k.fop("dve", lambda ks=ks: V.tensor_scalar(ks, ek, scol(18), None, ALU.mult))
        sB = R2f[:, f_sB:f_sB + 128]
        sC = R2f[:, f_sC:f_sC + 128]
        gsr = R2f[:, f_sD:f_sD + 128] if r == 0 else R2f[:, f_ep:f_ep + 128]
        k.fop("act", lambda: A_.activation(sB, kraw, AF.Abs))
        k.fop("act", lambda: A_.activation(sC, sB, AF.Exp, scale=-1.0))
        k.op("act", lambda: A_.activation(sB, sC, AF.Ln, bias=1.0), mark="act")
        k.wait("dve", "act")
        k.fop("dve", lambda: V.tensor_scalar_min(sC, kraw, 0.0))
        k.fop("dve", lambda gsr=gsr: V.tensor_sub(gsr, sC, sB))
        if own or halo:
            co = cosb[:, 512:576] if halo else cosb[:, t * 64:(t + 1) * 64]
            si = sinb[:, 512:576] if halo else sinb[:, t * 64:(t + 1) * 64]
            kr = R2f[:, f_kr:f_kr + 128] if r == 0 else R2f[:, f_sc:f_sc + 128]
            rope6(kraw, kr, co, si, sC, R2f[:, f_kr + 128:f_kr + 256])
        kdk[("s1", r)] = (k.cnt["dve"], k.cnt["act"])

    def k_rest(r, g, t, v_ap, chunk_base, own, halo):
        ks = (R2[:, c_ks:c_ks + 128] if r == 0 else R2[:, c_qs:c_qs + 128])
        gsr = R2f[:, f_sD:f_sD + 128] if r == 0 else R2f[:, f_ep:f_ep + 128]
        kr = R2f[:, f_kr:f_kr + 128] if r == 0 else R2f[:, f_sc:f_sc + 128]
        dv, ac = kdk[("s1", r)]
        k.wait("pe", "dve", max(dv, kdk.get("pf2rd", 0), kdk.get("pb0rd", 0)))
        k.op("pe", lambda gsr=gsr: P.transpose(PF[2][:, 0:128], gsr, idf[:]), mark="pe")
        if own:
            k.op("pe", lambda ks=ks: P.transpose(PB[0][:, 0:128], ks, idb[:]), mark="pe")
        if own or halo:
            k.op("pe", lambda kr=kr: P.transpose(PF[2][:, 256:384], kr, idf[:]), mark="pe")
        rt1 = k.cnt["pe"]
        k.wait("dve", "pe", rt1)
        gsT = R2f[:, f_kr + 128:f_kr + 256]
        Bc = R2f[:, f_B:f_B + 128]
        k.fop("dve", lambda: V.tensor_copy(gsT, PF[2][:, 0:128]))
        k.fop("dve", lambda: V.tensor_tensor_scan(
            Bc[:, 0:64], gsT[:, 0:64], zeros64[:], 0.0, ALU.add, ALU.add))
        k.fop("dve", lambda: V.tensor_tensor_scan(
            Bc[:, 64:128], gsT[:, 64:128], zeros64[:], 0.0, ALU.add, ALU.add))
        k.op("dve", lambda: V.tensor_scalar(scol(19), Bc[:, 63:64], G16, None, ALU.mult))
        k.fop("dve", lambda: V.tensor_scalar(scol(20), Bc[:, 127:128], G16, None, ALU.mult))
        if own or halo:
            if own:
                k.op("dve", lambda t=t: V.tensor_copy(
                    R2[:, c_kr + t * 128:c_kr + (t + 1) * 128], PF[2][:, 256:384]),
                    mark="dve")
            else:
                k.op("dve", lambda g=g: V.tensor_copy(
                    khalo[:, g * 64:(g + 1) * 64], PF[2][:, 320:384]), mark="dve")
        if halo:
            k.op("dve", lambda g=g, v_ap=v_ap: V.tensor_copy(
                vhalo[:, g * 128:(g + 1) * 128], v_ap), mark="dve")
        k.wait("act", "dve")
        enB = R2f[:, f_enB:f_enB + 128]
        dec = R2f[:, f_dec:f_dec + 128]
        if own:
            k.op("act", lambda t=t: A_.activation(
                R2[:, c_eb + t * 128:c_eb + (t + 1) * 128], Bc, AF.Exp, scale=G16))
            k.op("act", lambda: A_.activation(enB, Bc, AF.Exp, scale=-G16))
        k.op("act", lambda: A_.activation(dec[:, 0:64], Bc[:, 0:64], AF.Exp,
                                          bias=scol(19), scale=-G16))
        k.op("act", lambda: A_.activation(dec[:, 64:128], Bc[:, 64:128], AF.Exp,
                                          bias=scol(20), scale=-G16))
        k.op("act", lambda: A_.activation(scol(21), scol(19), AF.Exp))
        k.op("act", lambda: A_.activation(scol(22), scol(20), AF.Exp), mark="act")
        if own:
            k.wait("dve", "act")
            k.op("dve", lambda t=t: V.tensor_tensor(
                R2[:, c_kt + t * 128:c_kt + (t + 1) * 128], PB[0][:, 0:128], enB,
                ALU.mult), mark="dve")
            kdk["pb0rd"] = k.cnt["dve"]
        k.wait("pe", "act")
        k.wait("pe", "dve")
        k.op("pe", lambda: P.transpose(PF[2][:, 128:256], dec, idf[:]), mark="pe")
        rt2 = k.cnt["pe"]
        k.wait("dve", "pe", rt2)
        kh = R2[:, c_kh:c_kh + 128]
        k.op("dve", lambda ks=ks: V.tensor_tensor(kh, PF[2][:, 128:256], ks, ALU.mult),
             mark="dve")
        kdk["pf2rd"] = k.cnt["dve"]
        for cc in range(2):
            c = chunk_base + cc
            if own:
                k.op("dve", lambda c=c: V.tensor_copy(
                    R2[:, c_sb + c * 128:c_sb + (c + 1) * 128], Sf[:]), mark="dve")
            k.wait("pe", "dve")
            k.op("pe", lambda cc=cc, v_ap=v_ap: P.matmul(
                PF[3][:, 0:128], R2[64 * cc:64 * cc + 64, c_kh:c_kh + 128],
                v_ap[64 * cc:64 * cc + 64, :], start=True, stop=True), mark="pe")
            k.wait("dve", "pe")
            k.fop("dve", lambda cc=cc: V.scalar_tensor_tensor(
                Sf[:], Sf[:], scol(21 + cc), PF[3][:, 0:128], ALU.mult, ALU.add))

    k.dma("sq", cosb.ap().rearrange("p (t n) -> p t n", n=64)[:, 0:8, :],
          cosd.rearrange("(t p) n -> p t n", p=128), "tbl")
    k.dma("sq", sinb.ap().rearrange("p (t n) -> p t n", n=64)[:, 0:8, :],
          sind.rearrange("(t p) n -> p t n", p=128), "tbl")
    k.dma("sq", cosb[:, 512:576], cospd, "tbl")
    k.dma("sq", sinb[:, 512:576], sinpd, "tbl")
    k.dma("sq", trim[:], trimd, "tbl")
    k.dma("sq", swastd[:], swastdd, "tbl")
    k.dma("sq", swac0[:], swac0d, "tbl")
    k.dma("sq", idf[:], identd, "tbl")
    k.dma("gq", idb[:], identd, "gtbl")
    k.dma("sq", csw[:, 2:4], coefr, "tbl")
    k.dma("sq", csw[:, 4:6], scalr, "tbl")
    k.wait("dve", "tbl")
    k.wait("dve", "gtbl")
    k.fop("dve", lambda: V.tensor_mul(csw[:, 0:2], csw[:, 2:4], csw[:, 4:6]))
    k.fop("dve", lambda: V.memset(zeros64[:], 0.0))
    k.fop("dve", lambda: V.memset(Sf[:], 0.0))
    k.fop("dve", lambda: V.memset(oT[:], 0.0))
    k.barrier()

    for t in range(NT):
        xn_tok = R2[:, 8192:12288]
        ln_tile(x_own, t, xn_tok)
        transpose32(xn_tok, xnT, 1024, t * 128)
    k.barrier()
    if DEBUG:
        k.dma("sq", dbg_d[:, 0:32768], xnT[:], "oo")
        k.barrier()

    if ENABLE_ATTN:
        wslot0 = oT.ap()[:, 0:16384]
        for halfp in range(2):
            for th in range(4):
                t = halfp * 4 + th
                xn_tok = R2[:, 8192:12288]
                ln_tile(x_pfx, t, xn_tok)
                transpose32(xn_tok, ar1, 512, th * 128)
            k.barrier()
            evtokp = [0, 0]
            pend = []

            def p1_rest(item):
                r, g, t, v_ap = item
                k.fop("dve", lambda g=g: V.tensor_copy(Sf[:], Spfx[:, g * 128:(g + 1) * 128]))
                k_rest(r, g, t, v_ap, 0, False, t == 7)
                k.fop("dve", lambda g=g: V.tensor_copy(Spfx[:, g * 128:(g + 1) * 128], Sf[:]))

            for cb in range(4):
                wsrc = [Wv, Wv, Wk, Wk][cb]
                coff = [0, 512, 0, 512][cb]
                k.wait("gq", "pe")
                k.dma("gq", wslot0.rearrange("p (c n) -> p c n", n=512),
                      wsrc.rearrange("(c p) n -> p c n", p=128)[:, :, coff:coff + 512],
                      "wq")
                k.wait("pe", "wq")
                k.wait("pe", "dve")
                for th in range(4):
                    bank = PF[th % 2]
                    k.wait("pe", "dve", evtokp[th % 2])
                    for c in range(DCH):
                        k.op("pe", lambda bank=bank, c=c, th=th: P.matmul(
                            bank[:, 0:512],
                            ar1[:, c * 512 + th * 128:c * 512 + (th + 1) * 128],
                            wslot0.rearrange("p (c n) -> p c n", n=512)[:, c, :],
                            start=(c == 0), stop=(c == DCH - 1)),
                            mark="pe" if c == DCH - 1 else None)
                    pj = k.cnt["pe"]
                    k.wait("dve", "pe", pj)
                    if cb < 2:
                        evtokp[th % 2] = k.op("dve", lambda bank=bank, th=th, cb=cb: V.tensor_copy(
                            R2[:, th * 1024 + cb * 512:th * 1024 + cb * 512 + 512],
                            bank[:, 0:512]), mark="dve")
                    else:
                        t = halfp * 4 + th
                        for gg in range(4):
                            g = gg + (cb - 2) * 4
                            r = (th * 4 + gg) % 2
                            k.fop("dve", lambda bank=bank, gg=gg, r=r: V.tensor_copy(
                                kraw_ap(r), bank[:, gg * 128:(gg + 1) * 128]))
                            v_ap = R2[:, th * 1024 + g * 128:th * 1024 + (g + 1) * 128]
                            k_stage1(r, False, t == 7, t)
                            pend.append((r, g, t, v_ap))
                            if len(pend) > 1:
                                p1_rest(pend.pop(0))
                        evtokp[th % 2] = k.cnt["dve"]
            while pend:
                p1_rest(pend.pop(0))
            k.barrier()

        wring = [ar1.ap()[:, 0:8192], ar1.ap()[:, 8192:16384]]
        wtok = [0, 0]
        wsem = ["wr0", "wr1"]
        ring_i = [0]

        def wload(srcs):
            p = ring_i[0] % 2
            ring_i[0] += 1
            if wtok[p]:
                k.wait("gq", "pe", wtok[p])
            slot = wring[p]
            for src3, dcol, dw in srcs:
                k.dma("gq", slot.rearrange("p (c n) -> p c n", n=256)[:, :, dcol:dcol + dw],
                      src3, wsem[p])
            return p, slot

        Wq3 = Wq.rearrange("(c p) n -> p c n", p=128)
        Wk3 = Wk.rearrange("(c p) n -> p c n", p=128)
        Wv3 = Wv.rearrange("(c p) n -> p c n", p=128)

        def proj_tile(slot, t, width, bank):
            for c in range(DCH):
                k.op("pe", lambda bank=bank, c=c, t=t, slot=slot, width=width: P.matmul(
                    bank[:, 0:width],
                    xnT[:, c * 1024 + t * 128:c * 1024 + (t + 1) * 128],
                    slot.rearrange("p (c n) -> p c n", n=256)[:, c, 0:width],
                    start=(c == 0), stop=(c == DCH - 1)),
                    mark="pe" if c == DCH - 1 else None)

        pkv, skv = wload([(Wk3[:, :, 0:128], 0, 128), (Wv3[:, :, 0:128], 128, 128)])
        for g in range(KVH if ATTN_STAGE >= 2 else 0):
            k.fop("dve", lambda g=g: V.tensor_copy(Sf[:], Spfx[:, g * 128:(g + 1) * 128]))
            k.wait("pe", wsem[pkv])
            k.wait("pe", "dve")
            evtok = [0, 0]
            vaps = {}
            for t in range(NT):
                bank = PF[t % 2]
                k.wait("pe", "dve", evtok[t % 2])
                proj_tile(skv, t, 256, bank)
                pj = k.cnt["pe"]
                k.wait("dve", "pe", pj)
                r = t % 2
                k.fop("dve", lambda bank=bank, r=r: V.tensor_copy(
                    kraw_ap(r), bank[:, 0:128]))
                v_ap = R2[:, c_v + t * 128:c_v + (t + 1) * 128]
                vaps[t] = v_ap
                evtok[r] = k.op("dve", lambda bank=bank, v_ap=v_ap: V.tensor_copy(
                    v_ap, bank[:, 128:256]), mark="dve")
                k_stage1(r, True, False, t)
                if t >= 1:
                    k_rest((t - 1) % 2, g, t - 1, vaps[t - 1], 2 * (t - 1), True, False)
            k_rest(1, g, 7, vaps[7], 14, True, False)
            kv_done = k.cnt["pe"]
            wtok[pkv] = kv_done
            if ATTN_STAGE < 3:
                if g + 1 < KVH:
                    pkv, skv = wload([(Wk3[:, :, (g + 1) * 128:(g + 2) * 128], 0, 128),
                                      (Wv3[:, :, (g + 1) * 128:(g + 2) * 128], 128, 128)])
                continue
            pq0, sq0 = wload([(Wq3[:, :, g * 512:g * 512 + 256], 0, 256)])
            for hp in range(2 if ATTN_STAGE >= 3 else 0):
                psl, ssl = (pq0, sq0) if hp == 0 else wload(
                    [(Wq3[:, :, g * 512 + 256:g * 512 + 512], 0, 256)])
                k.wait("pe", wsem[psl])
                k.wait("pe", "dve")
                qd = {"pb1rd": k.cnt["dve"]}
                qevt = [0, 0]
                qrW = R2.ap()[:, c_qr:c_qr + 2048].rearrange("p (cc q) -> p cc q", q=128)

                def q_stage1(t):
                    r = t % 2
                    qraw = (R2f[:, f_qraw:f_qraw + 256] if r == 0
                            else R2f[:, 6528:6784])
                    bank = PF[t % 2]
                    k.fop("act", lambda bank=bank, qraw=qraw: A_.activation(
                        qraw, bank[:, 0:256], AF.Identity))
                    qevt[r] = k.cnt["act"]
                    k.wait("dve", "act")
                    k.op("dve", lambda qraw=qraw: V.reduce_max(
                        scol(24), qraw[:, 0:128], axis=AX, negate=True))
                    k.fop("dve", lambda qraw=qraw: V.reduce_max(
                        scol(25), qraw[:, 128:256], axis=AX, negate=True))
                    k.wait("act", "dve")
                    eq = R2f[:, f_B:f_B + 256]
                    k.op("act", lambda qraw=qraw, eq=eq: A_.activation(
                        eq[:, 0:128], qraw[:, 0:128], AF.Exp, bias=scol(24),
                        accum_out=scol(26)))
                    k.op("act", lambda qraw=qraw, eq=eq: A_.activation(
                        eq[:, 128:256], qraw[:, 128:256], AF.Exp, bias=scol(25),
                        accum_out=scol(27)), mark="act")
                    k.wait("dve", "act")
                    k.fop("dve", lambda: V.reciprocal(R2f[:, f_cols + 28:f_cols + 30],
                                                      R2f[:, f_cols + 26:f_cols + 28]))
                    k.fop("dve", lambda: V.tensor_scalar(
                        R2f[:, f_cols + 28:f_cols + 30],
                        R2f[:, f_cols + 28:f_cols + 30], 0.5, None, ALU.mult))
                    qs = (R2[:, c_qs:c_qs + 256] if r == 0 else R2[:, c_ks:c_ks + 256])
                    k.op("dve", lambda eq=eq, qs=qs: V.tensor_scalar(
                        qs[:, 0:128], eq[:, 0:128], scol(28), None, ALU.mult))
                    qd[("qs", t)] = k.op("dve", lambda eq=eq, qs=qs: V.tensor_scalar(
                        qs[:, 128:256], eq[:, 128:256], scol(29), None, ALU.mult),
                        mark="dve")
                    co = cosb[:, t * 64:(t + 1) * 64]
                    si = sinb[:, t * 64:(t + 1) * 64]
                    kr = (R2f[:, f_kr:f_kr + 256] if r == 0
                          else R2f[:, f_sA:f_sA + 256])
                    for h2 in range(2):
                        rope6(qraw[:, h2 * 128:(h2 + 1) * 128],
                              kr[:, h2 * 128:(h2 + 1) * 128], co, si,
                              R2f[:, f_sC:f_sC + 128], R2f[:, f_sD:f_sD + 128])
                    qd[("rope", t)] = k.cnt["dve"]

                def q_rest(t):
                    r = t % 2
                    qs = (R2[:, c_qs:c_qs + 256] if r == 0 else R2[:, c_ks:c_ks + 256])
                    kr = (R2f[:, f_kr:f_kr + 256] if r == 0
                          else R2f[:, f_sA:f_sA + 256])
                    k.wait("pe", "dve", max(qd[("qs", t)], qd["pb1rd"]))
                    for h2 in range(2):
                        k.op("pe", lambda h2=h2, qs=qs: P.transpose(
                            PB[1][:, h2 * 128:(h2 + 1) * 128],
                            qs[:, h2 * 128:(h2 + 1) * 128], idb[:]),
                            mark="pe" if h2 == 1 else None)
                    qtp = k.cnt["pe"]
                    k.wait("pe", "dve", max(qd[("rope", t)], qd.get("pf4rd", 0)))
                    for h2 in range(2):
                        k.op("pe", lambda h2=h2, kr=kr: P.transpose(
                            PF[4][:, h2 * 128:(h2 + 1) * 128],
                            kr[:, h2 * 128:(h2 + 1) * 128], idf[:]),
                            mark="pe" if h2 == 1 else None)
                    qrp = k.cnt["pe"]
                    k.wait("dve", "pe", qtp)
                    for h2 in range(2):
                        k.op("dve", lambda h2=h2, t=t: V.tensor_tensor(
                            R2[:, c_qt + h2 * 1024 + t * 128:c_qt + h2 * 1024 + (t + 1) * 128],
                            PB[1][:, h2 * 128:(h2 + 1) * 128],
                            R2[:, c_eb + t * 128:c_eb + (t + 1) * 128], ALU.mult),
                            mark="dve" if h2 == 1 else None)
                    qd["pb1rd"] = k.cnt["dve"]
                    k.wait("dve", "pe", qrp)
                    for h2 in range(2):
                        k.op("dve", lambda h2=h2, t=t: V.tensor_copy(
                            qrW[:, 2 * t:2 * t + 2, h2 * 64:(h2 + 1) * 64],
                            PF[4][:, h2 * 128:(h2 + 1) * 128].rearrange(
                                "p (a u) -> p a u", u=64)),
                            mark="dve" if h2 == 1 else None)
                    qd["pf4rd"] = k.cnt["dve"]

                for t in range(NT):
                    bank = PF[t % 2]
                    k.wait("pe", "act", qevt[t % 2])
                    proj_tile(ssl, t, 256, bank)
                    pj = k.cnt["pe"]
                    k.wait("act", "pe", pj)
                    q_stage1(t)
                    if t >= 1:
                        q_rest(t - 1)
                q_rest(7)
                wtok[psl] = k.cnt["pe"]
                if hp == 1 and g + 1 < KVH:
                    pkv, skv = wload([(Wk3[:, :, (g + 1) * 128:(g + 2) * 128], 0, 128),
                                      (Wv3[:, :, (g + 1) * 128:(g + 2) * 128], 128, 128)])
                qtv = R2.ap()[:, c_qt:c_qt + 2048].rearrange("p (a q) -> p a q", q=1024)
                qt3 = lambda c: qtv[:, :, c * 64:(c + 1) * 64]
                otv = oT.ap().rearrange("p (a q) -> p a q", q=1024)
                oev = {0: 0, 1: 0}
                tok = {}
                pbv = [R2[:, c_pb:c_pb + 128], R2[:, c_qs:c_qs + 128]]
                pTa = R2[:, c_pT:c_pT + 128]
                pTb = R2[:, c_pT + 128:c_pT + 256]
                Pb = R2[:, c_Pb:c_Pb + 128]

                qr3 = lambda c: R2[:, c_qr + c * 128:c_qr + (c + 1) * 128]

                def em_scores(c):
                    k.wait("pe", "dve", tok.get(("stt", c - 2), 0))
                    if c == 0:
                        k.op("pe", lambda g=g: P.matmul(
                            PF[0][:, 0:64], qr3(0),
                            khalo[:, g * 64:(g + 1) * 64], start=True, stop=True))
                        k.op("pe", lambda: P.matmul(
                            PF[0][:, 64:128], qr3(0),
                            R2[:, c_kr:c_kr + 64], start=True, stop=True), mark="pe")
                    else:
                        k.op("pe", lambda c=c: P.matmul(
                            PF[c % 2][:, 0:128], qr3(c),
                            R2[:, c_kr + (c - 1) * 64:c_kr + (c + 1) * 64],
                            start=True, stop=True), mark="pe")
                    tok[("scpe", c)] = k.cnt["pe"]

                def em_softmax(c):
                    k.wait("dve", "pe", tok[("scpe", c)])
                    if ("exp", c - 1) in tok:
                        k.wait("dve", "act", tok[("exp", c - 1)])
                    msk = swac0 if c == 0 else swastd
                    sc_ = R2f[:, f_sc:f_sc + 128]
                    k.fop("dve", lambda sc_=sc_, msk=msk, c=c: V.scalar_tensor_tensor(
                        sc_, PF[c % 2][:, 0:128], RSQ, msk[:], ALU.mult, ALU.add))
                    tok[("stt", c)] = k.cnt["dve"]
                    k.fop("dve", lambda sc_=sc_: V.reduce_max(
                        scol(32), sc_, axis=AX, negate=True))
                    k.wait("act", "dve")
                    ep = R2f[:, f_ep:f_ep + 128]
                    tok[("exp", c)] = k.op("act", lambda ep=ep, sc_=sc_: A_.activation(
                        ep, sc_, AF.Exp, bias=scol(32), accum_out=scol(33)), mark="act")
                    k.wait("dve", "act", tok[("exp", c)])
                    k.fop("dve", lambda: V.reciprocal(scol(34), scol(33)))
                    k.fop("dve", lambda: V.tensor_scalar(scol(34), scol(34), 0.5, None, ALU.mult))
                    tok[("p", c)] = k.op("dve", lambda c=c, ep=ep: V.tensor_scalar(
                        pbv[c % 2], ep, scol(34), None, ALU.mult), mark="dve")

                k.wait("pe", "dve")
                k.wait("pe", "act")
                em_scores(0)
                em_softmax(0)
                for c in range(16):
                    t = c // 2
                    cp = c % 2
                    if c + 1 < 16:
                        em_scores(c + 1)
                    k.wait("pe", "dve", max(tok[("p", c)], tok.get(("ptc", c - 1), 0)))
                    pb = pbv[c % 2]
                    if cp == 1:
                        k.op("pe", lambda pb=pb: P.transpose(PB[0][:, 0:128], pb, idb[:]),
                             mark="pe")
                    else:
                        k.op("pe", lambda pb=pb: P.transpose(
                            PB[0][64:128, 0:128], pb[:, 0:64], idb[:]))
                        k.op("pe", lambda pb=pb: P.transpose(
                            PB[0][0:64, 0:128], pb[:, 64:128], idb[:]), mark="pe")
                    tok[("ptpe", c)] = k.cnt["pe"]
                    k.wait("pe", "dve", tok.get(("mask", c - 1), 0))
                    klo = (c - 1) * 64 if cp == 1 else c * 64
                    k.op("pe", lambda c=c, klo=klo: P.matmul(
                        PF[4][:, 0:128],
                        R2[:, c_kt + klo:c_kt + klo + 128], qt3(c),
                        start=True, stop=True), mark="pe")
                    tok[("phpe", c)] = k.cnt["pe"]
                    k.wait("dve", "pe", tok[("ptpe", c)])
                    if cp == 1:
                        k.op("dve", lambda: V.tensor_copy(pTa[:], PB[0][:, 0:128]),
                             mark="dve")
                    else:
                        k.op("dve", lambda: V.memset(pTa[0:64, :], 0.0))
                        k.op("dve", lambda: V.tensor_copy(pTa[64:128, :], PB[0][64:128, 0:128]))
                        k.op("dve", lambda: V.tensor_copy(pTb[0:64, :], PB[0][0:64, 0:128]))
                        k.op("dve", lambda: V.memset(pTb[64:128, :], 0.0), mark="dve")
                    tok[("ptc", c)] = k.cnt["dve"]
                    k.wait("dve", "pe", tok[("phpe", c)])
                    trv = trim[:, 0:128] if cp == 0 else trim[:, 128:256]
                    tok[("mask", c)] = k.op("dve", lambda trv=trv: V.tensor_tensor(
                        Pb[:], PF[4][:, 0:128], trv, ALU.mult), mark="dve")
                    if c + 1 < 16:
                        em_softmax(c + 1)
                    ob = PF[2 + (c % 2)]
                    k.wait("pe", "dve", max(tok[("mask", c)], oev[c % 2], tok[("ptc", c)]))
                    vt = R2[:, c_v + t * 128:c_v + (t + 1) * 128]
                    k.op("pe", lambda ob=ob, vt=vt: P.matmul(
                        ob[:, 0:128], vt, Pb[:], start=True, stop=False))
                    k.op("pe", lambda ob=ob, c=c: P.matmul(
                        ob[:, 0:128], R2[:, c_sb + c * 128:c_sb + (c + 1) * 128], qt3(c),
                        start=False, stop=False))
                    if cp == 1:
                        k.op("pe", lambda ob=ob, vt=vt: P.matmul(
                            ob[:, 0:128], vt, pTa[:], start=False, stop=True), mark="pe")
                    else:
                        vprev = (vhalo[:, g * 128:(g + 1) * 128] if c == 0 else
                                 R2[:, c_v + (t - 1) * 128:c_v + t * 128])
                        k.op("pe", lambda ob=ob, vprev=vprev: P.matmul(
                            ob[:, 0:128], vprev, pTa[:], start=False, stop=False))
                        k.op("pe", lambda ob=ob, vt=vt: P.matmul(
                            ob[:, 0:128], vt, pTb[:], start=False, stop=True), mark="pe")
                    tok[("ope", c)] = k.cnt["pe"]
                    k.wait("dve", "pe", tok[("ope", c)])
                    oev[c % 2] = k.op("dve", lambda ob=ob, g=g, hp=hp, c=c: V.tensor_copy(
                        otv[:, 4 * g + 2 * hp:4 * g + 2 * hp + 2, c * 64:(c + 1) * 64],
                        ob[:, 0:128].rearrange("p (a b) -> p a b", b=64)), mark="dve")
        k.barrier()

    k.dma("gq", ar1.ap()[:, 0:8192].rearrange("p (c n) -> p c n", n=256),
          Aft.rearrange("(c p) n -> p c n", p=128), "wr0")
    c_xa = 12288
    k.wait("pe", "wr0")
    k.wait("pe", "dve")
    for e2 in range(2):
        for th in range(2):
            bank = PF[(e2 * 2 + th) % 2]
            k.wait("pe", "dve")
            for c in range(DCH):
                k.op("pe", lambda bank=bank, c=c, e2=e2, th=th: P.matmul(
                    bank[:, 0:512],
                    ar1[:, c * 256 + e2 * 128:c * 256 + e2 * 128 + 128],
                    xnT[:, c * 1024 + th * 512:c * 1024 + (th + 1) * 512],
                    start=(c == 0), stop=(c == DCH - 1)),
                    mark="pe" if c == DCH - 1 else None)
            k.wait("dve", "pe")
            k.op("dve", lambda bank=bank, e2=e2, th=th: V.tensor_scalar(
                R2[:, c_xa + e2 * 1024 + th * 512:c_xa + e2 * 1024 + (th + 1) * 512],
                bank[:, 0:512], csw[:, e2:e2 + 1], None, ALU.mult), mark="dve")
    k.barrier()

    wslotA = [ar1.ap()[:, 0:8192], ar1.ap()[:, 8192:16384]]
    bfslot = [smallb.ap()[:, 0:512], smallb.ap()[:, 512:1024]]
    bslot = [R2.ap()[:, 2048:2304], R2.ap()[:, 2304:2560]]
    wtokA = [0, 0]
    xring = [R2f[:, 0:256], R2f[:, 256:512]]
    ering = [R2f[:, 512:768], R2f[:, 768:1024]]
    xrtok = [0, 0]
    WT3 = WT.rearrange("(c p) n -> p c n", p=128)
    Bf3 = Bft.rearrange("(e p) n -> p e n", p=128)

    def load_wt(j):
        p = j % 2
        if wtokA[p]:
            k.wait("gq", "pe", wtokA[p])
        k.dma("gq", wslotA[p].rearrange("p (c n) -> p c n", n=256),
              WT3[:, :, j * 256:(j + 1) * 256], f"wr{p}")
        k.dma("gq", bfslot[p].rearrange("p (e n) -> p e n", n=256),
              Bf3[:, :, j * 256:(j + 1) * 256], f"wr{p}")
        k.dma("gq", bslot[p], biasbc[:, j * 256:(j + 1) * 256], f"wr{p}")

    load_wt(0)
    if DEBUG:
        k.wait("sq", "wr0")
        k.dma("sq", dbg_d[:, 32768:40960], wslotA[0], "oo")
        k.barrier()
    for j in range(16):
        p = j % 2
        if j + 1 < 16:
            load_wt(j + 1)
        k.wait("pe", f"wr{p}")
        for t in range(NT):
            bank = PF[2 + (t % 2)]
            if xrtok[t % 2]:
                k.wait("pe", "dve", xrtok[t % 2])
            for c in range(DCH):
                k.op("pe", lambda bank=bank, c=c, p=p, t=t: P.matmul(
                    bank[:, 0:256],
                    xnT[:, c * 1024 + t * 128:c * 1024 + (t + 1) * 128],
                    wslotA[p].rearrange("p (c n) -> p c n", n=256)[:, c, :],
                    start=(c == 0), stop=False))
            for e2 in range(2):
                k.op("pe", lambda bank=bank, e2=e2, p=p, t=t: P.matmul(
                    bank[:, 0:256],
                    R2[:, c_xa + e2 * 1024 + t * 128:c_xa + e2 * 1024 + (t + 1) * 128],
                    bfslot[p].rearrange("p (e n) -> p e n", n=256)[:, e2, :],
                    start=False, stop=(e2 == 1)),
                    mark="pe" if e2 == 1 else None)
            wtokA[p] = k.cnt["pe"]
            xp = t % 2
            if xrtok[xp]:
                k.wait("sq", "dve", xrtok[xp])
            k.dma("sq", xring[xp], x_own[t * 128:(t + 1) * 128, j * 256:(j + 1) * 256], "xq")
            k.wait("dve", "pe")
            k.wait("dve", "xq")
            k.fop("dve", lambda bank=bank, xp=xp: V.scalar_tensor_tensor(
                ering[xp], bank[:, 0:256], 1.0, xring[xp], ALU.mult, ALU.add))
            xrtok[xp] = k.op("dve", lambda xp=xp, p=p: V.tensor_tensor(
                ering[xp], ering[xp], bslot[p], ALU.add), mark="dve")
            k.wait("sq", "dve", xrtok[xp])
            k.dma("sq", scr_d[t * 128:(t + 1) * 128, j * 256:(j + 1) * 256], ering[xp], "so")
    k.barrier()

    woslot = [xnT.ap()[:, 0:16384], xnT.ap()[:, 16384:32768]]
    wotok = [0, 0]
    sring = [R2f[:, 0:512], R2f[:, 512:1024]]
    oring = [R2f[:, 2048:2560], R2f[:, 2560:3072]]
    srtok = [0, 0]
    Wo3 = Wo.rearrange("(c p) n -> p c n", p=128)

    def load_wo(j):
        p = j % 2
        if wotok[p]:
            k.wait("gq", "pe", wotok[p])
        k.dma("gq", woslot[p].rearrange("p (c n) -> p c n", n=512),
              Wo3[:, :, j * 512:(j + 1) * 512], f"wr{p}")

    load_wo(0)
    for j in range(8):
        p = j % 2
        if j + 1 < 8:
            load_wo(j + 1)
        k.wait("pe", f"wr{p}")
        for t in range(NT):
            bank = PF[2 + (t % 2)]
            if srtok[t % 2]:
                k.wait("pe", "dve", srtok[t % 2])
            for h in range(NH):
                k.op("pe", lambda bank=bank, h=h, p=p, t=t: P.matmul(
                    bank[:, 0:512],
                    oT[:, h * 1024 + t * 128:h * 1024 + (t + 1) * 128],
                    woslot[p].rearrange("p (c n) -> p c n", n=512)[:, h, :],
                    start=(h == 0), stop=(h == NH - 1)),
                    mark="pe" if h == NH - 1 else None)
            wotok[p] = k.cnt["pe"]
            rp = t % 2
            if srtok[rp]:
                k.wait("sq", "dve", srtok[rp])
            k.dma("sq", sring[rp], scr_d[t * 128:(t + 1) * 128, j * 512:(j + 1) * 512], "si")
            k.wait("dve", "pe")
            k.wait("dve", "si")
            srtok[rp] = k.op("dve", lambda bank=bank, rp=rp: V.scalar_tensor_tensor(
                oring[rp], bank[:, 0:512], 1.0, sring[rp], ALU.mult, ALU.add), mark="dve")
            k.wait("sq", "dve", srtok[rp])
            k.dma("sq", out_d[t * 128:(t + 1) * 128, j * 512:(j + 1) * 512], oring[rp], "oo")

    k.wait("sq", "oo")
    k.emit()
    return nc


_CACHE = {}


def kernel(**inputs):
    nc = _CACHE.get("nc")
    if nc is None:
        nc = build_program()
        _CACHE["nc"] = nc

    x = np.ascontiguousarray(np.asarray(inputs["L0_attn_hidden_states"], np.float32))
    Wq = np.ascontiguousarray(np.asarray(inputs["L0_attn_Wq"], np.float32))
    Wk = np.ascontiguousarray(np.asarray(inputs["L0_attn_Wk"], np.float32))
    Wv = np.ascontiguousarray(np.asarray(inputs["L0_attn_Wv"], np.float32))
    Wo = np.ascontiguousarray(np.asarray(inputs["L0_attn_Wo"], np.float32))
    WT = np.ascontiguousarray(np.asarray(inputs["L0_ffn_W"], np.float32).T)
    A = np.asarray(inputs["L0_ffn_A"], np.float32)
    Bf = np.asarray(inputs["L0_ffn_Bf"], np.float32)
    bias = np.asarray(inputs["L0_ffn_bias"], np.float32)
    coeffs = np.asarray(inputs["L0_ffn_coeffs"], np.float32)
    scales = np.asarray(inputs["L0_ffn_scales"], np.float32)

    Aft = np.ascontiguousarray(A.transpose(2, 0, 1).reshape(4096, 256))
    Bft = np.ascontiguousarray(Bf.transpose(0, 2, 1).reshape(256, 4096))
    biasbc = np.ascontiguousarray(np.tile(bias[None, :], (128, 1)))
    coefr = np.ascontiguousarray(np.repeat(coeffs, 16).reshape(2, 128).T)
    scalr = np.ascontiguousarray(np.repeat(scales, 16).reshape(2, 128).T)

    inv = (1.0 / (10000.0 ** (2.0 * np.arange(64, dtype=np.float64) / 128.0)))
    a = np.arange(64)[:, None]
    cc = np.arange(128)[None, :]
    rel = a + 64 - cc
    basem = (rel >= 0) & (rel <= 64)
    swastd = np.where(basem, 0.0, NEG).astype(np.float32)
    swastd = np.concatenate([swastd, swastd], axis=0)
    b0 = basem & (cc >= 64)
    swa_even = np.concatenate([np.where(b0, 0.0, NEG).astype(np.float32)] * 2, axis=0)
    tk = np.arange(64)[:, None]
    tq = np.arange(64)[None, :]
    tri = (tq >= tk).astype(np.float32)
    base = np.concatenate([tri, tri], axis=1)
    z64 = np.zeros((64, 128), np.float32)
    trimm = np.concatenate([np.concatenate([base, z64], axis=0),
                            np.concatenate([z64, base], axis=0)], axis=1)
    ident = np.eye(128, dtype=np.float32)
    zeros_pfx = np.zeros((1024, 4096), np.float32)
    zeros_tab = np.zeros((128, 64), np.float32)

    in_maps = []
    for c in range(8):
        b, half = c // 2, c % 2
        pos = (half * 1024 + np.arange(1024, dtype=np.float64))[:, None] * inv[None, :]
        cos = np.cos(pos).astype(np.float32)
        sin = np.sin(pos).astype(np.float32)
        if half == 1:
            ppos = (960 + np.arange(64, dtype=np.float64))[:, None] * inv[None, :]
            pc = np.concatenate([zeros_tab[0:64], np.cos(ppos).astype(np.float32)], axis=0)
            psn = np.concatenate([zeros_tab[0:64], np.sin(ppos).astype(np.float32)], axis=0)
        else:
            pc, psn = zeros_tab, zeros_tab
        xo = np.ascontiguousarray(x[b, half * 1024:(half + 1) * 1024, :])
        xp = np.ascontiguousarray(x[b, 0:1024, :]) if half == 1 else zeros_pfx
        in_maps.append({
            "x_own": xo, "x_pfx": xp,
            "Wq": Wq, "Wk": Wk, "Wv": Wv, "Wo": Wo, "WT": WT,
            "Aft": Aft, "Bft": Bft, "biasbc": biasbc,
            "coefr": coefr, "scalr": scalr,
            "cosd": np.ascontiguousarray(cos), "sind": np.ascontiguousarray(sin),
            "cospd": np.ascontiguousarray(pc), "sinpd": np.ascontiguousarray(psn),
            "trimd": trimm,
            "swastdd": swastd,
            "swac0d": swa_even if half == 0 else swastd,
            "identd": ident,
        })

    res = run_bass_kernel_spmd(nc, in_maps, core_ids=list(range(8)))
    if DEBUG:
        _CACHE["res"] = res
    out = np.empty((4, 2048, 4096), np.float32)
    for c in range(8):
        b, half = c // 2, c % 2
        out[b, half * 1024:(half + 1) * 1024, :] = res.results[c]["out"]
    return out
